# version 17
# baseline (speedup 1.0000x reference)
import numpy as np
import ml_dtypes
from contextlib import ExitStack
import concourse.bass as bass
import concourse.mybir as mybir
from concourse.bass_utils import run_bass_kernel_spmd

F32 = mybir.dt.float32
BF16 = mybir.dt.bfloat16
AF = mybir.ActivationFunctionType
ALU = mybir.AluOpType
AX = mybir.AxisListType.X

D = 2048
T = 2048
NT = T // 128
NJ = D // 128
NH = 16
HD = 64
EL, IL = 32, 768
ES, IS = 16, 128
LAMBDA_INIT = float(0.8 - 0.6 * np.exp(-0.3 * 12))
EPS = 1e-5

_DEBUG = False


class Tok:
    def __init__(self, sem):
        self.sem = sem
        self.n = 0

    def inc(self, amt):
        self.n += amt
        return (self.sem, self.n)


class Sched:
    ENGS = ("sync", "gpsimd", "vector", "scalar", "tensor")

    def __init__(self):
        self.items = {e: [] for e in self.ENGS}

    def emit(self, eng, fn, wait=(), inc=None, amt=1):
        tok = inc.inc(amt) if inc is not None else None
        rw = []
        for w in wait:
            if callable(w):
                w = w()
            if w is not None:
                rw.append(w)
        self.items[eng].append((fn, rw, [(tok[0], amt)] if tok else []))
        return tok

    def run_all(self, block):
        def runner(name):
            def run(e):
                for fn, waits, incs in self.items[name]:
                    for (sem, val) in waits:
                        e.wait_ge(sem, val)
                    ins = fn(e)
                    for (sem, a) in incs:
                        assert ins is not None
                        ins.then_inc(sem, a)
            return run
        block.sync(runner("sync"))
        block.gpsimd(runner("gpsimd"))
        block.vector(runner("vector"))
        block.scalar(runner("scalar"))
        block.tensor(runner("tensor"))


def build_program(debug=False):
    nc = bass.Bass("TRN2", target_bir_lowering=False)

    def din(name, shape, dt=F32):
        return nc.dram_tensor(name, shape, dt, kind="ExternalInput").ap()

    x0_d = din("x0", [T, D])
    wq_d = din("wq", [16, 128, D])
    wk_d = din("wk", [16, 128, D])
    wv_d = din("wv", [16, 128, D])
    wo_d = din("wo", [16, 128, D])
    gate_d = din("gate", [16, 128, 32])
    lam_d = din("lam", [1, 256])
    subln_d = din("subln", [128])
    eye_d = din("eye", [128, 128], BF16)
    eyef_d = din("eyef", [128, 128], F32)
    wg_d = din("wg", [EL, 6, 128, D])
    wu_d = din("wu", [EL, 6, 128, D])
    wd_d = din("wd", [EL, 4, 128, 6, 512])
    cwg_d = din("cwg", [ES, 128, D])
    cwu_d = din("cwu", [ES, 128, D])
    cwd_d = din("cwd", [ES, 4, 128, 512])

    out_d = nc.dram_tensor("out", [T, D], F32, kind="ExternalOutput").ap()
    sk = "ExternalOutput" if debug else "Internal"
    qkT_d = nc.dram_tensor("qkT", [2, NJ, 128, T], BF16, kind="Internal").ap()
    attnT_d = nc.dram_tensor("attnT", [NT, 128, NJ, 128], BF16, kind="Internal").ap()
    x1_d = nc.dram_tensor("x1", [T, D], F32, kind=sk).ap()
    dbg = {}
    if debug:
        dbg["lnT"] = nc.dram_tensor("dbg_lnT", [128, NJ * T], F32, kind="ExternalOutput").ap()
        dbg["v"] = nc.dram_tensor("dbg_v", [128, NT * D], F32, kind="ExternalOutput").ap()
        dbg["cw"] = nc.dram_tensor("dbg_cw", [128, NT * 32], F32, kind="ExternalOutput").ap()
        dbg["cws"] = nc.dram_tensor("dbg_cws", [128, NT * 16], F32, kind="ExternalOutput").ap()
        dbg["ln2T"] = nc.dram_tensor("dbg_ln2T", [128, NJ * T], F32, kind="ExternalOutput").ap()
        dbg["qkT"] = nc.dram_tensor("dbg_qkT", [2, NJ, 128, T], F32, kind="ExternalOutput").ap()
        dbg["A1"] = nc.dram_tensor("dbg_A1", [128, 512], F32, kind="ExternalOutput").ap()
        dbg["A2"] = nc.dram_tensor("dbg_A2", [128, 512], F32, kind="ExternalOutput").ap()
        dbg["r1"] = nc.dram_tensor("dbg_r1", [1, 512], F32, kind="ExternalOutput").ap()
        dbg["r2"] = nc.dram_tensor("dbg_r2", [1, 512], F32, kind="ExternalOutput").ap()
        dbg["p5"] = nc.dram_tensor("dbg_p5", [128, 1024], F32, kind="ExternalOutput").ap()
        dbg["cmb"] = nc.dram_tensor("dbg_cmb", [128, 512], F32, kind="ExternalOutput").ap()
        dbg["bc1"] = nc.dram_tensor("dbg_bc1", [128, 512], F32, kind="ExternalOutput").ap()
        dbg["cmb1"] = nc.dram_tensor("dbg_cmb1", [128, 512], F32, kind="ExternalOutput").ap()
        dbg["rb"] = nc.dram_tensor("dbg_rb", [1, 512], F32, kind="ExternalOutput").ap()
        dbg["cmb2"] = nc.dram_tensor("dbg_cmb2", [128, 512], F32, kind="ExternalOutput").ap()
        dbg["attnT"] = nc.dram_tensor("dbg_attnT", [NT, 128, NJ, 128], F32, kind="ExternalOutput").ap()

    ctx = ExitStack()
    sb = lambda n, s, d: ctx.enter_context(nc.sbuf_tensor(n, s, d))
    sem = lambda n: ctx.enter_context(nc.semaphore(n))

    LD = Tok(sem("ld"))
    LW = Tok(sem("lw"))
    ST = Tok(sem("st"))
    PE = Tok(sem("pe"))
    AC = Tok(sem("ac"))
    DV = Tok(sem("dv"))

    S = Sched()

    lnT = sb("lnT", [128, NJ * T], BF16)
    cw_sb = sb("cw_s_", [128, NT * 32], F32)
    cws_sb = sb("cws_s_", [128, NT * 16], F32)
    subln_c = sb("subln_cs", [128, 1], F32)
    eye = sb("eye_s", [128, 128], BF16)
    eyef = sb("eyef_s", [128, 128], F32)
    ones_bf = sb("ones_bf", [1, 128], BF16)
    ones_m1 = sb("ones_m1", [128, 1], BF16)
    ones33a = sb("ones33a", [128, 33], BF16)
    ones33b = sb("ones33b", [128, 33], BF16)
    gate_sb = sb("gate_s", [128, 16 * 32], F32)
    lam_sb = sb("lam_s", [1, 256], F32)
    lrow = sb("lrow", [1, 8], F32)
    eps_col = sb("eps_col", [128, 1], F32)

    t_eye = S.emit("sync", lambda e: e.dma_start(out=eye[:], in_=eye_d), inc=LD, amt=16)
    t_eyef = S.emit("sync", lambda e: e.dma_start(out=eyef[:], in_=eyef_d), inc=LD, amt=16)
    t_gate = S.emit("sync", lambda e: e.dma_start(
        out=gate_sb[:].rearrange("p (j c) -> p j c", c=32), in_=gate_d.transpose([1, 0, 2])),
        inc=LD, amt=16)
    t_lam = S.emit("sync", lambda e: e.dma_start(out=lam_sb[:], in_=lam_d), inc=LD, amt=16)
    t_sub = S.emit("sync", lambda e: e.dma_start(
        out=subln_c[:], in_=subln_d.rearrange("(p o) -> p o", o=1)), inc=LD, amt=16)
    S.emit("vector", lambda e: e.memset(eps_col[:], EPS))
    S.emit("vector", lambda e: e.memset(ones_bf[:], 1.0))
    S.emit("vector", lambda e: e.memset(ones_m1[:], 1.0))
    S.emit("vector", lambda e: e.memset(ones33a[:], 0.0))
    S.emit("vector", lambda e: e.memset(ones33b[:], 0.0))
    S.emit("vector", lambda e: e.memset(ones33a[:, 0:1], 1.0))
    t_ones = S.emit("vector", lambda e: e.memset(ones33b[:, 32:33], 1.0), inc=DV)
    tmp64 = sb("tmp64", [1, 64], F32)
    tl = S.emit("vector", lambda e: e.tensor_tensor(tmp64[:], lam_sb[0:1, 0:64],
                                                    lam_sb[0:1, 64:128], ALU.mult),
                wait=[t_lam], inc=DV)
    tl = S.emit("vector", lambda e: e.reduce_sum(lrow[0:1, 0:1], tmp64[:], axis=AX),
                wait=[tl], inc=DV)
    tl2 = S.emit("vector", lambda e: e.tensor_tensor(tmp64[:], lam_sb[0:1, 128:192],
                                                     lam_sb[0:1, 192:256], ALU.mult),
                 wait=[tl], inc=DV)
    tl2 = S.emit("vector", lambda e: e.reduce_sum(lrow[0:1, 1:2], tmp64[:], axis=AX),
                 wait=[tl2], inc=DV)
    ta = S.emit("scalar", lambda e: e.activation(lrow[0:1, 2:3], lrow[0:1, 0:1], AF.Exp),
                wait=[tl2], inc=AC)
    ta = S.emit("scalar", lambda e: e.activation(lrow[0:1, 3:4], lrow[0:1, 1:2], AF.Exp), inc=AC)
    tf_ = S.emit("vector", lambda e: e.tensor_tensor(lrow[0:1, 4:5], lrow[0:1, 3:4],
                                                     lrow[0:1, 2:3], ALU.subtract),
                 wait=[ta], inc=DV)
    t_lamf = S.emit("vector", lambda e: e.tensor_scalar(lrow[0:1, 4:5], lrow[0:1, 4:5],
                                                        -LAMBDA_INIT, None, ALU.add),
                    wait=[tf_], inc=DV)
    t_sube = S.emit("vector", lambda e: e.tensor_scalar(subln_c[:], subln_c[:],
                                                        1.0 - LAMBDA_INIT, None, ALU.mult),
                    wait=[t_sub], inc=DV)

    grpA = ExitStack()
    stats_ = grpA.enter_context(nc.sbuf_tensor("statsA", [128, 16], F32))
    xa = grpA.enter_context(nc.sbuf_tensor("xaA", [128, D], F32))
    xb = grpA.enter_context(nc.sbuf_tensor("xbA", [128, D], F32))
    lnbf_a = grpA.enter_context(nc.sbuf_tensor("lnbfA_a", [128, D], BF16))
    lnbf_b = grpA.enter_context(nc.sbuf_tensor("lnbfA_b", [128, D], BF16))
    psA = ExitStack()
    trp = [psA.enter_context(nc.psum_tensor(f"trpA{i}", [128, 1024], BF16)) for i in range(2)]
    prev_x_cons = [None, None]
    prev_ln_cons = [None, None]
    prev_tr = [None, None]
    prev_sq_read = [None, None]
    prev_rstd_read = [None, None]
    prev_lnb_rdA = [None, None]
    lnA_last = None
    for tt in range(NT):
        par = tt % 2
        stats = stats_[:, par * 8:(par + 1) * 8]
        xt = xa if par == 0 else xb
        lnb = lnbf_a if par == 0 else lnbf_b
        tld = S.emit("sync", lambda e, xt=xt, tt=tt: e.dma_start(
            out=xt[:], in_=x0_d[tt * 128:(tt + 1) * 128, :]),
            wait=[prev_x_cons[par]], inc=LD, amt=16)
        ts1 = S.emit("vector", lambda e, xt=xt, stats=stats: e.reduce_sum(
            stats[:, 0:1], xt[:], axis=AX),
            wait=[tld], inc=DV)
        ts2 = S.emit("scalar", lambda e, xt=xt, lnb=lnb, stats=stats: e.activation(
            lnb[:], xt[:], AF.Square, accum_out=stats[:, 1:2]),
            wait=[tld, prev_sq_read[par], prev_lnb_rdA[par]], inc=AC)
        tm = S.emit("vector", lambda e, stats=stats: e.tensor_scalar(
            stats[:, 2:3], stats[:, 0:1], 1.0 / D, None, ALU.mult),
            wait=[ts1, ts2], inc=DV)
        tm = S.emit("vector", lambda e, stats=stats: e.tensor_tensor(
            stats[:, 3:4], stats[:, 2:3], stats[:, 2:3], ALU.mult), wait=[tm], inc=DV)
        tm = S.emit("vector", lambda e, stats=stats: e.scalar_tensor_tensor(
            stats[:, 4:5], stats[:, 1:2], 1.0 / D, stats[:, 3:4], ALU.mult, ALU.subtract),
            wait=[tm], inc=DV)
        prev_sq_read[par] = tm
        tv = S.emit("scalar", lambda e, stats=stats: e.activation(
            stats[:, 5:6], stats[:, 4:5], AF.Ln, bias=eps_col[:]),
            wait=[tm, prev_rstd_read[par]], inc=AC)
        tv = S.emit("scalar", lambda e, stats=stats: e.activation(
            stats[:, 6:7], stats[:, 5:6], AF.Exp, scale=-0.5), inc=AC)
        tn = S.emit("vector", lambda e, xt=xt, lnb=lnb, stats=stats: e.tensor_scalar(
            lnb[:], xt[:], stats[:, 2:3], stats[:, 6:7], ALU.subtract, ALU.mult),
            wait=[tv, prev_ln_cons[par]], inc=DV)
        prev_rstd_read[par] = tn
        prev_x_cons[par] = tn
        packs = []
        for half in range(2):
            pr = trp[half]
            def tr_pack(e, lnb=lnb, pr=pr, half=half):
                ins = None
                for k in range(8):
                    j = half * 8 + k
                    ins = e.transpose(pr[:, k * 128:(k + 1) * 128],
                                      lnb[:, j * 128:(j + 1) * 128], eye[:])
                return ins
            tp = S.emit("tensor", tr_pack, wait=[tn, t_eye, prev_tr[half]], inc=PE)
            if half == 1:
                prev_lnb_rdA[par] = tp
            def ev(e, pr=pr, half=half, tt=tt):
                dst = lnT[:].rearrange("p (j t) -> p j t", t=T)[
                    :, half * 8:half * 8 + 8, tt * 128:(tt + 1) * 128]
                src = pr[:].rearrange("p (k c) -> p k c", c=128)
                return e.tensor_copy(dst, src)
            te = S.emit("vector", ev, wait=[tp], inc=DV)
            prev_tr[half] = te
            packs.append(te)
        prev_ln_cons[par] = packs[1]
        lnA_last = packs[1]
    t_lnA_done = lnA_last
    if debug:
        S.emit("gpsimd", lambda e: e.dma_start(out=dbg["lnT"], in_=lnT[:]),
               wait=[t_lnA_done], inc=ST, amt=16)

    psA.close()
    grpA.close()
    grpB = ExitStack()
    v_sb = grpB.enter_context(nc.sbuf_tensor("v_sb", [128, NT * D], BF16))
    wsl = [grpB.enter_context(nc.sbuf_tensor(f"wslB{i}", [128, D], BF16)) for i in range(2)]
    otile = [grpB.enter_context(nc.sbuf_tensor(f"otB{i}", [128, D], BF16)) for i in range(2)]
    psB = ExitStack()
    qp0 = psB.enter_context(nc.psum_tensor("qpB0", [128, D], F32))
    trB = [psB.enter_context(nc.psum_tensor(f"trB{i}", [128, 1024], BF16)) for i in range(2)]

    def qkv_proj(w_dram, which, prev_state):
        prev_w, prev_p, prev_o, prev_trB = prev_state
        last = None
        for i in range(NJ):
            par = i % 2
            tw = S.emit("gpsimd", lambda e, i=i, par=par, w_dram=w_dram: e.dma_start(
                out=wsl[par][:], in_=w_dram[i]),
                wait=[prev_w[par], t_lnA_done if prev_w[par] is None else None], inc=LW, amt=16)
            def mms(e, par=par):
                ins = None
                for j in range(NJ):
                    st = wsl[par][:, j * 128:(j + 1) * 128]
                    for c in range(4):
                        ins = e.matmul(qp0[:, c * 512:(c + 1) * 512], st,
                                       lnT[:, j * T + c * 512:j * T + (c + 1) * 512],
                                       start=(j == 0), stop=(j == NJ - 1),
                                       skip_group_check=True)
                return ins
            tp = S.emit("tensor", mms, wait=[tw, t_lnA_done, prev_p[0]], inc=PE)
            prev_w[par] = tp
            tev = S.emit("vector", lambda e, par=par: e.tensor_copy(otile[par][:], qp0[:]),
                         wait=[tp, prev_o[par]], inc=DV)
            prev_p[0] = tev
            if which < 2:
                tsp = S.emit("sync", lambda e, i=i, par=par, which=which: e.dma_start(
                    out=qkT_d[which, i], in_=otile[par][:]), wait=[tev], inc=ST, amt=16)
                prev_o[par] = tsp
                last = tsp
            else:
                toks = []
                for half in range(2):
                    pr = trB[half]
                    def trp_(e, par=par, pr=pr, half=half):
                        ins = None
                        for k in range(8):
                            s = half * 8 + k
                            ins = e.transpose(pr[:, k * 128:(k + 1) * 128],
                                              otile[par][:, s * 128:(s + 1) * 128], eye[:])
                        return ins
                    tpp = S.emit("tensor", trp_, wait=[tev, prev_trB[half]], inc=PE)
                    def evv(e, pr=pr, half=half, i=i):
                        dst = v_sb[:].rearrange("p (s c) -> p s c", c=D)[
                            :, half * 8:half * 8 + 8, i * 128:(i + 1) * 128]
                        src = pr[:].rearrange("p (k c) -> p k c", c=128)
                        return e.tensor_copy(dst, src)
                    tvv = S.emit("vector", evv, wait=[tpp], inc=DV)
                    prev_trB[half] = tvv
                    toks.append(tvv)
                prev_o[par] = toks[1]
                last = toks[1]
        return last

    stB = [[None, None], [None, None], [None, None], [None, None]]
    t_q = qkv_proj(wq_d, 0, stB)
    t_k = qkv_proj(wk_d, 1, stB)
    t_v = qkv_proj(wv_d, 2, stB)
    if debug:
        S.emit("gpsimd", lambda e: e.dma_start(out=dbg["v"], in_=v_sb[:]),
               wait=[t_v], inc=ST, amt=16)
        S.emit("gpsimd", lambda e: e.dma_start(out=dbg["qkT"], in_=qkT_d),
               wait=[t_q, t_k], inc=ST, amt=16)
    psB.close()

    qh = [grpB.enter_context(nc.sbuf_tensor(f"qh{i}", [128, T], BF16)) for i in range(2)]
    kh = [grpB.enter_context(nc.sbuf_tensor(f"kh{i}", [128, T], BF16)) for i in range(2)]
    probs = [grpB.enter_context(nc.sbuf_tensor(f"pr{i}", [128, 1024], BF16)) for i in range(2)]
    A_sb = [grpB.enter_context(nc.sbuf_tensor(f"Asb{i}", [128, 512], BF16)) for i in range(2)]
    rrow_a = grpB.enter_context(nc.sbuf_tensor("rrow_a", [1, 512], BF16))
    rrow_b = grpB.enter_context(nc.sbuf_tensor("rrow_b", [1, 512], BF16))
    t1row = grpB.enter_context(nc.sbuf_tensor("t1row", [1, 512], F32))
    t2row = grpB.enter_context(nc.sbuf_tensor("t2row", [1, 512], F32))
    comb1 = grpB.enter_context(nc.sbuf_tensor("comb1", [128, 512], F32))
    comb2 = grpB.enter_context(nc.sbuf_tensor("comb2", [128, 512], BF16))
    sqc = grpB.enter_context(nc.sbuf_tensor("sqc", [128, 512], BF16))
    rmsrow = grpB.enter_context(nc.sbuf_tensor("rmsrow", [1, 512], F32))
    attn_out = [grpB.enter_context(nc.sbuf_tensor(f"attn_out{i}", [128, 512], BF16))
                for i in range(2)]
    lnlam = grpB.enter_context(nc.sbuf_tensor("lnlam", [1, 1], F32))

    t_lnl = S.emit("vector", lambda e: e.tensor_scalar(lnlam[:], lrow[0:1, 4:5], -1.0, None,
                                                       ALU.mult), wait=[t_lamf], inc=DV)
    t_lnl = S.emit("scalar", lambda e: e.activation(lnlam[:], lnlam[:], AF.Ln), wait=[t_lnl], inc=AC)

    psC = ExitStack()
    SpA = psC.enter_context(nc.psum_tensor("SpA", [128, 1024], F32))
    SpB = psC.enter_context(nc.psum_tensor("SpB", [128, 1024], F32))
    A1p = psC.enter_context(nc.psum_tensor("A1p", [128, 512], F32))
    A2p = psC.enter_context(nc.psum_tensor("A2p", [128, 512], F32))
    SUMp = psC.enter_context(nc.psum_tensor("SUMp", [33, 512], F32))
    EPIp = psC.enter_context(nc.psum_tensor("EPIp", [128, 512], F32))

    qk_scale = float(HD) ** -0.5
    prev_qh = [None, None]
    prev_kh = [None, None]
    last_S_of_head = None
    PH = None
    P2H = [None, None]
    epi_sched = []
    t_C_done_hold = [None]

    for h in range(NH):
        hp = h % 2
        tqh = S.emit("sync", lambda e, h=h, hp=hp: e.dma_start(out=qh[hp][:], in_=qkT_d[0, h]),
                     wait=[prev_qh[hp], t_q if h < 2 else None], inc=LD, amt=16)
        tkh = S.emit("sync", lambda e, h=h, hp=hp: e.dma_start(out=kh[hp][:], in_=qkT_d[1, h]),
                     wait=[prev_kh[hp], t_k if h < 2 else None], inc=LD, amt=16)
        for c in range(4):
            t0 = c * 512
            first = (h == 0 and c == 0)
            op = (h * 4 + c) % 2
            cur_exp = [None] * 16
            cur_pv = [None] * 16
            pmap = {}
            for slot, fn in epi_sched:
                pmap.setdefault(slot, []).append(fn)
            epi_sched = []
            for st in range(17):
                if st < 16:
                    par = st % 2
                    Sp = SpA if par == 0 else SpB
                    if st >= 2:
                        w_bank = cur_exp[st - 2]
                    else:
                        w_bank = PH["exp"][14 + st] if PH is not None else None
                    def smm(e, hp=hp, st=st, t0=t0, Sp=Sp):
                        e.matmul(Sp[:, 0:512], kh[hp][0:64, st * 128:(st + 1) * 128],
                                 qh[hp][0:64, t0:t0 + 512], start=True, stop=True,
                                 skip_group_check=True)
                        return e.matmul(Sp[:, 512:1024], kh[hp][64:128, st * 128:(st + 1) * 128],
                                        qh[hp][64:128, t0:t0 + 512], start=True, stop=True,
                                        skip_group_check=True)
                    tS = S.emit("tensor", smm, wait=[tqh, tkh, w_bank], inc=PE)
                    last_S_of_head = tS
                if st >= 1:
                    sv = st - 1
                    parv = sv % 2
                    def pv(e, h=h, sv=sv, pb=probs[parv]):
                        v_st = v_sb[:, sv * D + h * 128:sv * D + (h + 1) * 128]
                        e.matmul(A1p[:], v_st, pb[:, 0:512],
                                 start=(sv == 0), stop=(sv == 15), skip_group_check=True)
                        e.matmul(A2p[:], v_st, pb[:, 512:1024],
                                 start=(sv == 0), stop=(sv == 15), skip_group_check=True)
                        e.matmul(SUMp[:], ones33a[:], pb[:, 0:512],
                                 start=(sv == 0), stop=False, skip_group_check=True)
                        return e.matmul(SUMp[:], ones33b[:], pb[:, 512:1024],
                                        start=False, stop=(sv == 15), skip_group_check=True)
                    wpv = [cur_exp[sv]]
                    if sv == 0:
                        if PH is not None:
                            wpv += [lambda H=PH: H["ev2"][0], lambda H=PH: H["lg2"][0]]
                        if first:
                            wpv += [t_ones, t_v]
                    tP = S.emit("tensor", pv, wait=wpv, inc=PE)
                    cur_pv[sv] = tP
                if st < 16:
                    par = st % 2
                    Sp = SpA if par == 0 else SpB
                    if st >= 2:
                        w_pb = cur_pv[st - 2]
                    elif PH is not None:
                        w_pb = PH["pv14"] if st == 0 else PH["pv15"]
                    else:
                        w_pb = None
                    tE = S.emit("scalar", lambda e, Sp=Sp, pb=probs[par]: e.activation(
                        pb[:], Sp[:], AF.Exp, scale=qk_scale),
                        wait=[w_pb, last_S_of_head], inc=AC)
                    cur_exp[st] = tE
                for fn in pmap.get(st, []):
                    fn()
            H = {k: [None] for k in ("ev1", "ev2", "lg1", "lg2", "ra", "rb", "b1", "c1",
                                     "b2", "c2", "cb", "sq", "qq", "ms", "mc", "br",
                                     "fin", "sp")}
            H["exp"] = cur_exp
            H["pv14"] = cur_pv[14]
            H["pv15"] = cur_pv[15]
            t_pv15 = cur_pv[15]
            e_first = first
            OP = PH
            SPH = P2H[op]

            def reg(slot, eng, fn, waits, hkey, incq, amt=1, H=H):
                hold = H[hkey]
                def run():
                    rw = [(w() if callable(w) else w) for w in waits]
                    hold[0] = S.emit(eng, fn, wait=rw, inc=incq, amt=amt)
                epi_sched.append((slot, run))

            reg(0, "vector", lambda e: e.tensor_copy(A_sb[0][:], A1p[:]),
                [t_pv15], "ev1", DV)
            reg(0, "vector", lambda e: e.tensor_copy(A_sb[1][:], A2p[:]),
                [], "ev2", DV)
            reg(0, "scalar", lambda e: e.activation(t1row[:], SUMp[0:1, :], AF.Ln),
                [t_pv15], "lg1", AC)
            reg(0, "scalar", lambda e: e.activation(t2row[:], SUMp[32:33, :], AF.Ln),
                [], "lg2", AC)
            reg(1, "scalar", lambda e: e.activation(rrow_a[:], t1row[:], AF.Exp, scale=-1.0),
                [(lambda H=OP: H["br"][0]) if OP is not None else None], "ra", AC)
            reg(1, "scalar", lambda e: e.activation(rrow_b[:], t2row[:], AF.Exp, scale=-1.0,
                                                    bias=lnlam[:]),
                [(lambda H=OP: H["b2"][0]) if OP is not None else None,
                 t_lnl if e_first else None], "rb", AC)
            reg(3, "tensor", lambda e: e.matmul(EPIp[:], ones_bf[:], rrow_a[:], start=True,
                                                stop=True, skip_group_check=True),
                [lambda H=H: H["ra"][0],
                 (lambda H=OP: H["fin"][0]) if OP is not None else None], "b1", PE)
            reg(4, "vector", lambda e: e.tensor_tensor(comb1[:], A_sb[0][:], EPIp[:], ALU.mult),
                [lambda H=H: H["b1"][0]], "c1", DV)
            reg(5, "tensor", lambda e: e.matmul(EPIp[:], ones_bf[:], rrow_b[:], start=True,
                                                stop=True, skip_group_check=True),
                [lambda H=H: H["rb"][0], lambda H=H: H["c1"][0]], "b2", PE)
            reg(6, "vector", lambda e: e.tensor_tensor(comb2[:], A_sb[1][:], EPIp[:], ALU.mult),
                [lambda H=H: H["b2"][0]], "c2", DV)
            reg(6, "vector", lambda e: e.tensor_tensor(comb1[:], comb1[:], comb2[:],
                                                       ALU.subtract),
                [lambda H=H: H["c2"][0]], "cb", DV)
            reg(7, "scalar", lambda e: e.activation(sqc[:], comb1[:], AF.Square),
                [lambda H=H: H["cb"][0],
                 (lambda H=OP: H["qq"][0]) if OP is not None else None], "sq", AC)
            reg(8, "tensor", lambda e: e.matmul(EPIp[0:1, :], ones_m1[:], sqc[:], start=True,
                                                stop=True, skip_group_check=True),
                [lambda H=H: H["sq"][0], lambda H=H: H["c2"][0]], "qq", PE)
            reg(9, "vector", lambda e: e.tensor_scalar(rmsrow[:], EPIp[0:1, :], 1.0 / 128,
                                                       None, ALU.mult),
                [lambda H=H: H["qq"][0]], "ms", DV)
            reg(9, "scalar", lambda e: e.activation(rmsrow[:], rmsrow[:], AF.Ln,
                                                    bias=eps_col[0:1, :]),
                [lambda H=H: H["ms"][0]], "ms", AC)
            reg(9, "scalar", lambda e: e.activation(rmsrow[:], rmsrow[:], AF.Exp, scale=-0.5),
                [], "ms", AC)
            reg(10, "vector", lambda e: e.tensor_copy(rrow_a[:], rmsrow[:]),
                [lambda H=H: H["ms"][0], lambda H=H: H["b1"][0]], "mc", DV)
            reg(12, "tensor", lambda e: e.matmul(EPIp[:], ones_bf[:], rrow_a[:], start=True,
                                                 stop=True, skip_group_check=True),
                [lambda H=H: H["mc"][0], lambda H=H: H["c1"][0]], "br", PE)
            ao = attn_out[op]
            reg(13, "vector", lambda e, ao=ao: e.scalar_tensor_tensor(
                ao[:], EPIp[:], subln_c[:], comb1[:], ALU.mult, ALU.mult),
                [lambda H=H: H["br"][0], t_sube if e_first else None,
                 (lambda H=SPH: H["sp"][0]) if SPH is not None else None], "fin", DV)

            def f_spill(e, h=h, c=c, ao=ao):
                dst = attnT_d[c * 4:(c + 1) * 4, :, h, :].transpose([1, 0, 2])
                srcp = ao[:].rearrange("p (k t) -> p k t", t=128)
                return e.dma_start(out=dst, in_=srcp)
            reg(14, "sync", f_spill, [lambda H=H: H["fin"][0]], "sp", ST, amt=16)
            t_C_done_hold[0] = H["sp"]

            PH = H
            P2H[op] = H
        prev_qh[hp] = last_S_of_head
        prev_kh[hp] = last_S_of_head
    for slot, fn in sorted(epi_sched, key=lambda x: x[0]):
        fn()
    epi_sched = []
    t_C_done = t_C_done_hold[0][0]
    assert t_C_done is not None
    psC.close()

    grpB.close()
    grpD = ExitStack()
    wo_sb = grpD.enter_context(nc.sbuf_tensor("wo_sb", [128, NJ * D], BF16))
    att_in = [grpD.enter_context(nc.sbuf_tensor(f"attin{i}", [128, D], BF16)) for i in range(2)]
    x1sb = [grpD.enter_context(nc.sbuf_tensor(f"x1sb{i}", [128, D], F32)) for i in range(2)]
    ln2f = [grpD.enter_context(nc.sbuf_tensor(f"ln2f{i}", [128, D], F32)) for i in range(2)]
    ln2Tf = grpD.enter_context(nc.sbuf_tensor("ln2Tf", [128, 1024], F32))
    logit_sb = grpD.enter_context(nc.sbuf_tensor("logit_sb", [128, 32], F32))
    rw_sb = grpD.enter_context(nc.sbuf_tensor("rw_sb", [128, 32], F32))
    rmask = grpD.enter_context(nc.sbuf_tensor("rmask", [128, 32], F32))
    rtmp = grpD.enter_context(nc.sbuf_tensor("rtmp", [128, 32], F32))
    rcol = grpD.enter_context(nc.sbuf_tensor("rcol", [128, 4], F32))
    statsD_ = grpD.enter_context(nc.sbuf_tensor("statsD", [128, 16], F32))
    xa = grpD.enter_context(nc.sbuf_tensor("xaD", [128, D], F32))
    xb = grpD.enter_context(nc.sbuf_tensor("xbD", [128, D], F32))
    lnbf_a = grpD.enter_context(nc.sbuf_tensor("lnbfD_a", [128, D], BF16))
    lnbf_b = grpD.enter_context(nc.sbuf_tensor("lnbfD_b", [128, D], BF16))

    two_pref = []
    tw_prev = t_C_done
    for j in range(NJ):
        tw = S.emit("gpsimd", lambda e, j=j: e.dma_start(
            out=wo_sb[:, j * D:(j + 1) * D], in_=wo_d[j]),
            wait=[tw_prev if j == 0 else None], inc=LW, amt=16)
        two_pref.append(tw)

    psD = ExitStack()
    wopsH = [psD.enter_context(nc.psum_tensor(f"wopsH{i}", [128, 1024], F32)) for i in range(2)]
    trD = psD.enter_context(nc.psum_tensor("trD", [128, 2048], BF16))
    trDf = psD.enter_context(nc.psum_tensor("trDf", [128, 512], F32))
    logp = psD.enter_context(nc.psum_tensor("logp", [128, 32], F32))

    prev_att = [None, None]
    prev_x0_cons = [None, None]
    prev_x1_cons = [None, None]
    prev_x1_dma = [None, None]
    prev_wops = [None, None]
    prev_trD = [None, None]
    prev_trDf = None
    prev_routmm = None
    prev_cw = None
    prev_sq_read = [None, None]
    prev_rstd_read = [None, None]
    prev_ln2f_cons = [None, None]
    prev_lnb_rd = [None, None]
    t_ln2_done = None
    t_router_pe_final = None
    stA = {}

    for tt in range(NT + 1):
        par = tt % 2
        if tt < NT:
            tat = S.emit("sync", lambda e, tt=tt, par=par: e.dma_start(
                out=att_in[par][:].rearrange("p (ch t) -> p ch t", t=128), in_=attnT_d[tt]),
                wait=[prev_att[par], t_C_done], inc=LD, amt=16)
            xt = xa if par == 0 else xb
            tx0 = S.emit("sync", lambda e, tt=tt, xt=xt: e.dma_start(
                out=xt[:], in_=x0_d[tt * 128:(tt + 1) * 128, :]),
                wait=[prev_x0_cons[par], t_C_done if tt < 2 else None], inc=LD, amt=16)
            x1t = x1sb[par]
            lnb = lnbf_a if par == 0 else lnbf_b
            tx1h = [None, None]
            for hw_ in range(2):
                def wom(e, par=par, hw_=hw_):
                    ins = None
                    for cc in range(NJ):
                        st = att_in[par][:, cc * 128:(cc + 1) * 128]
                        for c in range(2):
                            cg = hw_ * 2 + c
                            ins = e.matmul(wopsH[hw_][:, c * 512:(c + 1) * 512], st,
                                           wo_sb[:, cc * D + cg * 512:cc * D + (cg + 1) * 512],
                                           start=(cc == 0), stop=(cc == NJ - 1),
                                           skip_group_check=True)
                    return ins
                tp = S.emit("tensor", wom, wait=[tat, two_pref[-1], prev_wops[hw_]], inc=PE)
                if hw_ == 1:
                    prev_att[par] = tp
                tx1 = S.emit("vector", lambda e, xt=xt, x1t=x1t, hw_=hw_: e.tensor_tensor(
                    x1t[:, hw_ * 1024:(hw_ + 1) * 1024], xt[:, hw_ * 1024:(hw_ + 1) * 1024],
                    wopsH[hw_][:], ALU.add),
                    wait=[tp, tx0 if hw_ == 0 else None,
                          prev_x1_cons[par] if hw_ == 0 else None,
                          prev_x1_dma[par] if hw_ == 0 else None], inc=DV)
                prev_wops[hw_] = tx1
                tx1h[hw_] = tx1
            tx1 = tx1h[1]
            prev_x0_cons[par] = tx1
            tsx = S.emit("sync", lambda e, x1t=x1t, tt=tt: e.dma_start(
                out=x1_d[tt * 128:(tt + 1) * 128, :], in_=x1t[:]), wait=[tx1], inc=ST, amt=16)
            prev_x1_dma[par] = tsx
            stats = statsD_[:, par * 8:(par + 1) * 8]
            ts1 = S.emit("vector", lambda e, x1t=x1t, stats=stats: e.reduce_sum(
                stats[:, 0:1], x1t[:], axis=AX),
                wait=[tx1], inc=DV)
            ts2 = S.emit("scalar", lambda e, x1t=x1t, lnb=lnb, stats=stats: e.activation(
                lnb[:], x1t[:], AF.Square, accum_out=stats[:, 1:2]),
                wait=[tx1, prev_sq_read[par], prev_lnb_rd[par]], inc=AC)
            tm = S.emit("vector", lambda e, stats=stats: e.tensor_scalar(
                stats[:, 2:3], stats[:, 0:1], 1.0 / D, None, ALU.mult),
                wait=[ts1, ts2], inc=DV)
            tm = S.emit("vector", lambda e, stats=stats: e.tensor_tensor(
                stats[:, 3:4], stats[:, 2:3], stats[:, 2:3], ALU.mult), wait=[tm], inc=DV)
            tm = S.emit("vector", lambda e, stats=stats: e.scalar_tensor_tensor(
                stats[:, 4:5], stats[:, 1:2], 1.0 / D, stats[:, 3:4], ALU.mult, ALU.subtract),
                wait=[tm], inc=DV)
            prev_sq_read[par] = tm
            tv = S.emit("scalar", lambda e, stats=stats: e.activation(
                stats[:, 5:6], stats[:, 4:5], AF.Ln, bias=eps_col[:]),
                wait=[tm, prev_rstd_read[par]], inc=AC)
            tv = S.emit("scalar", lambda e, stats=stats: e.activation(
                stats[:, 6:7], stats[:, 5:6], AF.Exp, scale=-0.5), inc=AC)
            l2f = ln2f[par]
            tfl = S.emit("vector", lambda e, x1t=x1t, l2f=l2f, stats=stats: e.tensor_scalar(
                l2f[:], x1t[:], stats[:, 2:3], stats[:, 6:7], ALU.subtract, ALU.mult),
                wait=[tv, prev_ln2f_cons[par]], inc=DV)
            prev_rstd_read[par] = tfl
            prev_x1_cons[par] = tfl
            tn = S.emit("vector", lambda e, lnb=lnb, l2f=l2f: e.tensor_copy(lnb[:], l2f[:]),
                        wait=[tfl], inc=DV)
            stA[tt] = (tfl, tn, lnb, l2f)
        if tt >= 1:
            tb = tt - 1
            parb = tb % 2
            tfl, tn, lnb, l2f = stA.pop(tb)
            packs = []
            for half in range(2):
                def tr_pack(e, lnb=lnb, half=half):
                    ins = None
                    for k in range(8):
                        j = half * 8 + k
                        ins = e.transpose(trD[:, half * 1024 + k * 128:half * 1024 + (k + 1) * 128],
                                          lnb[:, j * 128:(j + 1) * 128], eye[:])
                    return ins
                tpd = S.emit("tensor", tr_pack, wait=[tn, prev_trD[half],
                                                      t_eye if tb == 0 and half == 0 else None],
                             inc=PE)
                def ev(e, half=half, tb=tb):
                    dst = lnT[:].rearrange("p (j t) -> p j t", t=T)[
                        :, half * 8:half * 8 + 8, tb * 128:(tb + 1) * 128]
                    src = trD[:, half * 1024:(half + 1) * 1024].rearrange(
                        "p (k c) -> p k c", c=128)
                    return e.tensor_copy(dst, src)
                te = S.emit("vector", ev, wait=[tpd], inc=DV)
                prev_trD[half] = te
                if half == 1:
                    prev_lnb_rd[parb] = tpd
                packs.append(te)
            t_ln2_done = packs[1]
            rmmA = None
            trt = None
            for q4 in range(4):
                def trf(e, q4=q4, l2f=l2f):
                    ins = None
                    for k in range(4):
                        j = q4 * 4 + k
                        ins = e.transpose(trDf[:, k * 128:(k + 1) * 128],
                                          l2f[:, j * 128:(j + 1) * 128], eyef[:])
                    return ins
                tpf = S.emit("tensor", trf, wait=[tfl, t_eyef if (tb == 0 and q4 == 0) else None,
                                                  prev_trDf], inc=PE)
                tef = S.emit("vector", lambda e, q4=q4: e.tensor_copy(
                    ln2Tf[:, (q4 % 2) * 512:(q4 % 2 + 1) * 512], trDf[:]),
                    wait=[tpf, prev_routmm if q4 < 2 else rmmA], inc=DV)
                prev_trDf = tef
                if q4 % 2 == 1:
                    def routmm(e, q4=q4):
                        ins = None
                        for jj in range(8):
                            j = (q4 // 2) * 8 + jj
                            ins = e.matmul(logp[:], ln2Tf[:, jj * 128:(jj + 1) * 128],
                                           gate_sb[:, j * 32:(j + 1) * 32],
                                           start=(j == 0), stop=(j == NJ - 1),
                                           skip_group_check=True)
                        return ins
                    trt = S.emit("tensor", routmm,
                                 wait=[tef, t_gate if tb == 0 and q4 == 1 else None,
                                       prev_routmm if q4 == 1 else None], inc=PE)
                    if q4 == 1:
                        rmmA = trt
                if q4 == 3:
                    prev_ln2f_cons[parb] = tef
            t_router_pe_final = trt
            tlg = S.emit("vector", lambda e: e.tensor_copy(logit_sb[:], logp[:]), wait=[trt], inc=DV)
            prev_routmm = tlg
            trw = S.emit("scalar", lambda e: e.activation(rw_sb[:], logit_sb[:], AF.Exp),
                         wait=[tlg], inc=AC)
            tmk = S.emit("vector", lambda e: e.tensor_copy(rtmp[:], logit_sb[:]), wait=[tlg], inc=DV)
            for k in range(3):
                tmk = S.emit("vector", lambda e, k=k: e.reduce_max(rcol[:, k:k + 1], rtmp[:], axis=AX),
                             wait=[tmk], inc=DV)
                tmk = S.emit("vector", lambda e, k=k: e.tensor_scalar(
                    rmask[:], rtmp[:], rcol[:, k:k + 1], None, ALU.is_ge), wait=[tmk], inc=DV)
                tmk = S.emit("vector", lambda e: e.scalar_tensor_tensor(
                    rtmp[:], rmask[:], -1e30, rtmp[:], ALU.mult, ALU.add), wait=[tmk], inc=DV)
            tmk = S.emit("vector", lambda e: e.reduce_max(rcol[:, 3:4], rtmp[:], axis=AX),
                         wait=[tmk], inc=DV)
            tmk = S.emit("vector", lambda e: e.tensor_scalar(
                rmask[:], logit_sb[:], rcol[:, 3:4], None, ALU.is_ge), wait=[tmk], inc=DV)
            tcw = S.emit("vector", lambda e, tb=tb: e.tensor_tensor(
                cw_sb[:, tb * 32:(tb + 1) * 32], rw_sb[:], rmask[:], ALU.mult),
                wait=[tmk, trw], inc=DV)
            tcw = S.emit("vector", lambda e, tb=tb: e.reduce_sum(
                rcol[:, 0:1], cw_sb[:, tb * 32:(tb + 1) * 32], axis=AX), wait=[tcw], inc=DV)
            tcw = S.emit("vector", lambda e: e.reciprocal(rcol[:, 1:2], rcol[:, 0:1]),
                         wait=[tcw], inc=DV)
            tcw = S.emit("vector", lambda e, tb=tb: e.tensor_scalar(
                cw_sb[:, tb * 32:(tb + 1) * 32], cw_sb[:, tb * 32:(tb + 1) * 32],
                rcol[:, 1:2], None, ALU.mult), wait=[tcw], inc=DV)
            def cws(e, tb=tb):
                v4 = cw_sb[:].rearrange("p (tq c g) -> p tq c g", c=16, g=2)
                return e.tensor_tensor(cws_sb[:, tb * 16:(tb + 1) * 16],
                                       v4[:, tb, :, 0], v4[:, tb, :, 1], ALU.add)
            tcs = S.emit("vector", cws, wait=[tcw], inc=DV)
            tcs = S.emit("vector", lambda e, tb=tb: e.tensor_scalar(
                cws_sb[:, tb * 16:(tb + 1) * 16], cws_sb[:, tb * 16:(tb + 1) * 16],
                0.05, None, ALU.mult), wait=[tcs], inc=DV)
            prev_cw = tcs
    psD.close()
    if debug:
        S.emit("gpsimd", lambda e: e.dma_start(out=dbg["ln2T"], in_=lnT[:]),
               wait=[t_ln2_done], inc=ST, amt=16)
        S.emit("sync", lambda e: e.dma_start(out=dbg["cw"], in_=cw_sb[:]),
               wait=[prev_cw], inc=ST, amt=16)
        S.emit("sync", lambda e: e.dma_start(out=dbg["cws"], in_=cws_sb[:]),
               wait=[prev_cw], inc=ST, amt=16)
        S.emit("gpsimd", lambda e: e.dma_start(out=dbg["attnT"], in_=attnT_d),
               wait=[t_C_done], inc=ST, amt=16)

    grpD.close()
    grpE = ExitStack()
    acc = grpE.enter_context(nc.sbuf_tensor("acc", [128, 8 * D], F32))
    a_sb = [grpE.enter_context(nc.sbuf_tensor(f"a_sb{i}", [128, 6 * 1024], BF16)) for i in range(2)]
    wg_s = [grpE.enter_context(nc.sbuf_tensor(f"wg_s{i}", [128, D], BF16)) for i in range(2)]
    wu_s = [grpE.enter_context(nc.sbuf_tensor(f"wu_s{i}", [128, D], BF16)) for i in range(2)]
    wd_s = [grpE.enter_context(nc.sbuf_tensor(f"wd_s{i}", [128, 2 * 6 * 512], BF16)) for i in range(2)]
    sg_sb = grpE.enter_context(nc.sbuf_tensor("sg_sb", [128, 1024], BF16))

    psE = ExitStack()
    pg = psE.enter_context(nc.psum_tensor("pgE", [128, 1024], F32))
    pu = psE.enter_context(nc.psum_tensor("puE", [128, 1024], F32))
    dn = [psE.enter_context(nc.psum_tensor(f"dn{i}", [128, 1024], F32)) for i in range(2)]

    t_out_final = []
    first_E = [True]
    prev_wg = [None, None]
    prev_wu = [None, None]
    prev_wd = [None, None]
    prev_a = [None, None]
    prev_mult = [None]
    prev_dn = [None, None]
    last_evac_h = [None]
    qctr = [0]
    pending = []
    prev_wd_emit = [None, None]

    def flush(n):
        for _ in range(n):
            if not pending:
                break
            fn, marker = pending.pop(0)
            fn()

    def fire_wd():
        for hp2 in range(2):
            if prev_wd_emit[hp2] is not None:
                prev_wd_emit[hp2]()
                prev_wd_emit[hp2] = None

    for half in range(2):
        thalf = half * 1024
        acc_init = []
        for k in range(8):
            tt = half * 8 + k
            ti = S.emit("sync", lambda e, k=k, tt=tt: e.dma_start(
                out=acc[:, k * D:(k + 1) * D], in_=x1_d[tt * 128:(tt + 1) * 128, :]),
                wait=[prev_cw if k == 0 else None,
                      t_out_final[-1] if (k == 0 and t_out_final) else None], inc=LD, amt=16)
            acc_init.append(ti)
        t_acc_ready = acc_init[-1]

        for ee in range(EL + ES):
            big = ee < EL
            epar = ee % 2
            n_i = 6 if big else 1
            ncols = 6 * 512 if big else 512
            nch = n_i
            npend = len(pending)
            t_a_parts = []
            my_twd = [None, None]
            def mk_wd(hp2, ee=ee, big=big, ncols=ncols, my_twd=my_twd):
                def emit_wd():
                    for hci in range(2):
                        hc = hp2 * 2 + hci
                        srcd = wd_d[ee, hc] if big else cwd_d[ee - EL, hc]
                        my_twd[hp2] = S.emit("gpsimd", lambda e, srcd=srcd, hp2=hp2,
                                             hci=hci, ncols=ncols: e.dma_start(
                            out=wd_s[hp2][:, hci * 3072:hci * 3072 + ncols], in_=srcd),
                            wait=[prev_wd[hp2] if hci == 0 else None], inc=LW, amt=16)
                return emit_wd
            my_wd_emit = [mk_wd(0), mk_wd(1)]
            chi = 0
            for ih in range(n_i):
                wpar = (ih % 2) if big else (ee % 2)
                srcg = wg_d[ee, ih] if big else cwg_d[ee - EL]
                srcu = wu_d[ee, ih] if big else cwu_d[ee - EL]
                first_wait = [t_router_pe_final, prev_cw] if first_E[0] else []
                twg = S.emit("gpsimd", lambda e, srcg=srcg, wpar=wpar: e.dma_start(
                    out=wg_s[wpar][:], in_=srcg),
                    wait=[prev_wg[wpar]] + first_wait, inc=LW, amt=16)
                twu = S.emit("gpsimd", lambda e, srcu=srcu, wpar=wpar: e.dma_start(
                    out=wu_s[wpar][:], in_=srcu), wait=[prev_wu[wpar]], inc=LW, amt=16)
                first_E[0] = False
                if ih == min(1, n_i - 1):
                    fire_wd()
                qctr[0] += 1
                def gu(e, wpar=wpar, thalf=thalf):
                    ins = None
                    for hh in range(NJ):
                        stg = wg_s[wpar][:, hh * 128:(hh + 1) * 128]
                        stu = wu_s[wpar][:, hh * 128:(hh + 1) * 128]
                        for cc in range(2):
                            mv = lnT[:, hh * T + thalf + cc * 512:hh * T + thalf + (cc + 1) * 512]
                            e.matmul(pg[:, cc * 512:(cc + 1) * 512], stg, mv,
                                     start=(hh == 0), stop=(hh == NJ - 1),
                                     skip_group_check=True)
                        for cc in range(2):
                            mv = lnT[:, hh * T + thalf + cc * 512:hh * T + thalf + (cc + 1) * 512]
                            ins = e.matmul(pu[:, cc * 512:(cc + 1) * 512], stu, mv,
                                           start=(hh == 0), stop=(hh == NJ - 1),
                                           skip_group_check=True)
                    return ins
                tgu = S.emit("tensor", gu, wait=[twg, twu, prev_mult[0],
                                                 t_ln2_done if qctr[0] == 1 and half == 0 else None],
                             inc=PE)
                tsl = S.emit("scalar", lambda e: e.activation(
                    sg_sb[:], pg[:], AF.Silu), wait=[tgu], inc=AC)
                tmu = S.emit("vector", lambda e, epar=epar, ih=ih: e.tensor_tensor(
                    a_sb[epar][:, ih * 1024:(ih + 1) * 1024], sg_sb[:], pu[:], ALU.mult),
                    wait=[tsl, prev_a[epar] if ih == 0 else None], inc=DV)
                prev_mult[0] = tmu
                t_a_parts.append(tmu)
                chi += 1
                if chi >= 2:
                    tgt = min(npend, ((chi - 1) * npend + nch - 2) // max(1, nch - 1))
                    flush(tgt - (npend - len(pending)))
                prev_wg[wpar] = tgu
                prev_wu[wpar] = tgu
            flush(len(pending))
            fire_wd()
            t_a = t_a_parts[-1]
            t_a_pre = t_a_parts[-2] if big else None
            cw_col = cw_sb if big else cws_sb
            cwi = ee if big else ee - EL
            ncol = 32 if big else 16
            newpend = []
            for hp2 in range(2):
                for k in range(8):
                    dpar = k % 2
                    split = big and k < 2
                    marker = hp2 if k == 7 else None
                    def unit(e=None, epar=epar, hp2=hp2, k=k, n_i=n_i, dpar=dpar, big=big,
                             t_a=t_a, t_a_pre=t_a_pre, my_twd=my_twd, split=split,
                             cw_col=cw_col, cwi=cwi, ncol=ncol, half=half,
                             ee=ee, t_acc_ready=t_acc_ready):
                        twd = my_twd[hp2]
                        def down(e, ih_lo=0, ih_hi=None):
                            if ih_hi is None:
                                ih_hi = n_i
                            ins = None
                            for ih in range(ih_lo, ih_hi):
                                st_a = a_sb[epar][:, ih * 1024 + k * 128:ih * 1024 + (k + 1) * 128]
                                e.matmul(dn[dpar][:, 0:512], st_a,
                                         wd_s[hp2][:, ih * 512:(ih + 1) * 512],
                                         start=(ih == 0), stop=(ih == n_i - 1),
                                         skip_group_check=True)
                                ins = e.matmul(dn[dpar][:, 512:1024], st_a,
                                               wd_s[hp2][:, 3072 + ih * 512:3072 + (ih + 1) * 512],
                                               start=(ih == 0), stop=(ih == n_i - 1),
                                               skip_group_check=True)
                            return ins
                        if split:
                            S.emit("tensor", lambda e: down(e, ih_lo=0, ih_hi=n_i - 1),
                                   wait=[t_a_pre, twd, prev_dn[dpar]], inc=PE)
                            tdn = S.emit("tensor", lambda e: down(e, ih_lo=n_i - 1),
                                         wait=[t_a], inc=PE)
                        else:
                            tdn = S.emit("tensor", down, wait=[t_a, twd, prev_dn[dpar]], inc=PE)
                        tev = S.emit("vector", lambda e: e.scalar_tensor_tensor(
                            acc[:, k * D + hp2 * 1024:k * D + (hp2 + 1) * 1024], dn[dpar][:],
                            cw_col[:, (half * 8 + k) * ncol + cwi:(half * 8 + k) * ncol + cwi + 1],
                            acc[:, k * D + hp2 * 1024:k * D + (hp2 + 1) * 1024],
                            ALU.mult, ALU.add),
                            wait=[tdn, t_acc_ready if (ee == 0 and hp2 == 0) else None], inc=DV)
                        prev_dn[dpar] = tev
                        last_evac_h[0] = tev
                        prev_wd[hp2] = tev
                        prev_a[epar] = tev
                    newpend.append((unit, marker))
            pending = newpend
            prev_wd_emit = my_wd_emit
        fire_wd()
        flush(len(pending))
        for k in range(8):
            tt = half * 8 + k
            to = S.emit("sync", lambda e, k=k, tt=tt: e.dma_start(
                out=out_d[tt * 128:(tt + 1) * 128, :], in_=acc[:, k * D:(k + 1) * D]),
                wait=[last_evac_h[0] if k == 0 else None], inc=ST, amt=16)
            t_out_final.append(to)
    psE.close()

    S.emit("sync", lambda e: e.wait_ge(ST.sem, ST.n))

    with nc.Block() as block:
        S.run_all(block)
    grpE.close()
    ctx.close()
    return nc


def _prep_inputs(L0_attn_x, L0_attn_wq, L0_attn_wk, L0_attn_wv, L0_attn_wo,
                 L0_attn_lambda_q1, L0_attn_lambda_k1, L0_attn_lambda_q2, L0_attn_lambda_k2,
                 L0_attn_subln_w, L0_ffn_gate_w, L0_ffn_Wg, L0_ffn_Wu, L0_ffn_Wd,
                 L0_ffn_cWg, L0_ffn_cWu, L0_ffn_cWd):
    f32 = lambda a: np.asarray(a, dtype=np.float32)

    def pre_qk(w):
        return np.ascontiguousarray(
            f32(w).reshape(16, 128, 16, 128).transpose(0, 3, 2, 1).reshape(16, 128, D))

    def pre_mov(w):
        return np.ascontiguousarray(f32(w).T.reshape(16, 128, D))

    wq = pre_qk(L0_attn_wq)
    wk = pre_qk(L0_attn_wk)
    wv = pre_qk(L0_attn_wv)
    wo = pre_mov(L0_attn_wo)
    gate = np.ascontiguousarray(f32(L0_ffn_gate_w).T.reshape(16, 128, 32))
    lam = np.concatenate([f32(L0_attn_lambda_q1), f32(L0_attn_lambda_k1),
                          f32(L0_attn_lambda_q2), f32(L0_attn_lambda_k2)]).reshape(1, 256)
    wg = np.ascontiguousarray(
        f32(L0_ffn_Wg).reshape(EL, 16, 128, 6, 128).transpose(0, 3, 2, 1, 4).reshape(EL, 6, 128, D))
    wu = np.ascontiguousarray(
        f32(L0_ffn_Wu).reshape(EL, 16, 128, 6, 128).transpose(0, 3, 2, 1, 4).reshape(EL, 6, 128, D))
    wd = np.ascontiguousarray(
        f32(L0_ffn_Wd).reshape(EL, 6, 128, 4, 512).transpose(0, 3, 2, 1, 4))
    cwg = np.ascontiguousarray(
        f32(L0_ffn_cWg).reshape(ES, 16, 128, 128).transpose(0, 2, 1, 3).reshape(ES, 128, D))
    cwu = np.ascontiguousarray(
        f32(L0_ffn_cWu).reshape(ES, 16, 128, 128).transpose(0, 2, 1, 3).reshape(ES, 128, D))
    cwd = np.ascontiguousarray(
        f32(L0_ffn_cWd).reshape(ES, 128, 4, 512).transpose(0, 2, 1, 3))
    common = dict(
        wq=wq, wk=wk, wv=wv, wo=wo, gate=gate, lam=lam,
        subln=f32(L0_attn_subln_w),
        eye=np.eye(128).astype(ml_dtypes.bfloat16),
        eyef=np.eye(128).astype(np.float32),
        wg=wg, wu=wu, wd=wd, cwg=cwg, cwu=cwu, cwd=cwd,
    )
    xs = f32(L0_attn_x)
    in_maps = []
    for c in range(8):
        m = dict(common)
        m["x0"] = np.ascontiguousarray(xs[c])
        in_maps.append(m)
    return in_maps


def kernel(**inputs):
    in_maps = _prep_inputs(**inputs)
    nc = build_program(debug=_DEBUG)
    res = run_bass_kernel_spmd(nc, in_maps, core_ids=list(range(8)))
    out = np.stack([np.asarray(r["out"], dtype=np.float32) for r in res.results])
    if _DEBUG:
        kernel._last = res
    return out


# version 18
# speedup vs baseline: 1.1033x; 1.1033x over previous
import numpy as np
import ml_dtypes
from contextlib import ExitStack
import concourse.bass as bass
import concourse.mybir as mybir
from concourse.bass_utils import run_bass_kernel_spmd

F32 = mybir.dt.float32
BF16 = mybir.dt.bfloat16
AF = mybir.ActivationFunctionType
ALU = mybir.AluOpType
AX = mybir.AxisListType.X

D = 2048
T = 2048
NT = T // 128
NJ = D // 128
NH = 16
HD = 64
EL, IL = 32, 768
ES, IS = 16, 128
LAMBDA_INIT = float(0.8 - 0.6 * np.exp(-0.3 * 12))
EPS = 1e-5

_DEBUG = False


class Tok:
    def __init__(self, sem):
        self.sem = sem
        self.n = 0

    def inc(self, amt):
        self.n += amt
        return (self.sem, self.n)


class Sched:
    ENGS = ("sync", "gpsimd", "vector", "scalar", "tensor")

    def __init__(self):
        self.items = {e: [] for e in self.ENGS}

    def emit(self, eng, fn, wait=(), inc=None, amt=1):
        tok = inc.inc(amt) if inc is not None else None
        rw = []
        for w in wait:
            if callable(w):
                w = w()
            if w is not None:
                rw.append(w)
        self.items[eng].append((fn, rw, [(tok[0], amt)] if tok else []))
        return tok

    def run_all(self, block):
        def runner(name):
            def run(e):
                for fn, waits, incs in self.items[name]:
                    for (sem, val) in waits:
                        e.wait_ge(sem, val)
                    ins = fn(e)
                    for (sem, a) in incs:
                        assert ins is not None
                        ins.then_inc(sem, a)
            return run
        block.sync(runner("sync"))
        block.gpsimd(runner("gpsimd"))
        block.vector(runner("vector"))
        block.scalar(runner("scalar"))
        block.tensor(runner("tensor"))


def build_program(debug=False):
    nc = bass.Bass("TRN2", target_bir_lowering=False)

    def din(name, shape, dt=F32):
        return nc.dram_tensor(name, shape, dt, kind="ExternalInput").ap()

    x0_d = din("x0", [T, D])
    wq_d = din("wq", [16, 128, D])
    wk_d = din("wk", [16, 128, D])
    wv_d = din("wv", [16, 128, D])
    wo_d = din("wo", [16, 128, D])
    gate_d = din("gate", [16, 128, 32])
    lam_d = din("lam", [1, 256])
    subln_d = din("subln", [128])
    eye_d = din("eye", [128, 128], BF16)
    eyef_d = din("eyef", [128, 128], F32)
    wg_d = din("wg", [EL, 6, 128, D])
    wu_d = din("wu", [EL, 6, 128, D])
    wd_d = din("wd", [EL, 4, 128, 6, 512])
    cwg_d = din("cwg", [ES, 128, D])
    cwu_d = din("cwu", [ES, 128, D])
    cwd_d = din("cwd", [ES, 4, 128, 512])

    out_d = nc.dram_tensor("out", [T, D], F32, kind="ExternalOutput").ap()
    sk = "ExternalOutput" if debug else "Internal"
    qkT_d = nc.dram_tensor("qkT", [2, NJ, 128, T], BF16, kind="Internal").ap()
    attnT_d = nc.dram_tensor("attnT", [NT, 128, NJ, 128], BF16, kind="Internal").ap()
    x1_d = nc.dram_tensor("x1", [T, D], F32, kind=sk).ap()
    dbg = {}
    if debug:
        dbg["lnT"] = nc.dram_tensor("dbg_lnT", [128, NJ * T], F32, kind="ExternalOutput").ap()
        dbg["v"] = nc.dram_tensor("dbg_v", [128, NT * D], F32, kind="ExternalOutput").ap()
        dbg["cw"] = nc.dram_tensor("dbg_cw", [128, NT * 32], F32, kind="ExternalOutput").ap()
        dbg["cws"] = nc.dram_tensor("dbg_cws", [128, NT * 16], F32, kind="ExternalOutput").ap()
        dbg["ln2T"] = nc.dram_tensor("dbg_ln2T", [128, NJ * T], F32, kind="ExternalOutput").ap()
        dbg["qkT"] = nc.dram_tensor("dbg_qkT", [2, NJ, 128, T], F32, kind="ExternalOutput").ap()
        dbg["A1"] = nc.dram_tensor("dbg_A1", [128, 512], F32, kind="ExternalOutput").ap()
        dbg["A2"] = nc.dram_tensor("dbg_A2", [128, 512], F32, kind="ExternalOutput").ap()
        dbg["r1"] = nc.dram_tensor("dbg_r1", [1, 512], F32, kind="ExternalOutput").ap()
        dbg["r2"] = nc.dram_tensor("dbg_r2", [1, 512], F32, kind="ExternalOutput").ap()
        dbg["p5"] = nc.dram_tensor("dbg_p5", [128, 1024], F32, kind="ExternalOutput").ap()
        dbg["cmb"] = nc.dram_tensor("dbg_cmb", [128, 512], F32, kind="ExternalOutput").ap()
        dbg["bc1"] = nc.dram_tensor("dbg_bc1", [128, 512], F32, kind="ExternalOutput").ap()
        dbg["cmb1"] = nc.dram_tensor("dbg_cmb1", [128, 512], F32, kind="ExternalOutput").ap()
        dbg["rb"] = nc.dram_tensor("dbg_rb", [1, 512], F32, kind="ExternalOutput").ap()
        dbg["cmb2"] = nc.dram_tensor("dbg_cmb2", [128, 512], F32, kind="ExternalOutput").ap()
        dbg["attnT"] = nc.dram_tensor("dbg_attnT", [NT, 128, NJ, 128], F32, kind="ExternalOutput").ap()

    ctx = ExitStack()
    sb = lambda n, s, d: ctx.enter_context(nc.sbuf_tensor(n, s, d))
    sem = lambda n: ctx.enter_context(nc.semaphore(n))

    LD = Tok(sem("ld"))
    LW = Tok(sem("lw"))
    ST = Tok(sem("st"))
    PE = Tok(sem("pe"))
    AC = Tok(sem("ac"))
    DV = Tok(sem("dv"))

    S = Sched()

    lnT = sb("lnT", [128, NJ * T], BF16)
    cw_sb = sb("cw_s_", [128, NT * 32], F32)
    cws_sb = sb("cws_s_", [128, NT * 16], F32)
    subln_c = sb("subln_cs", [128, 1], F32)
    eye = sb("eye_s", [128, 128], BF16)
    eyef = sb("eyef_s", [128, 128], F32)
    ones_bf = sb("ones_bf", [1, 128], BF16)
    ones_m1 = sb("ones_m1", [128, 1], BF16)
    ones33a = sb("ones33a", [128, 33], BF16)
    ones33b = sb("ones33b", [128, 33], BF16)
    gate_sb = sb("gate_s", [128, 16 * 32], F32)
    lam_sb = sb("lam_s", [1, 256], F32)
    lrow = sb("lrow", [1, 8], F32)
    eps_col = sb("eps_col", [128, 1], F32)

    t_eye = S.emit("sync", lambda e: e.dma_start(out=eye[:], in_=eye_d), inc=LD, amt=16)
    t_eyef = S.emit("sync", lambda e: e.dma_start(out=eyef[:], in_=eyef_d), inc=LD, amt=16)
    t_gate = S.emit("sync", lambda e: e.dma_start(
        out=gate_sb[:].rearrange("p (j c) -> p j c", c=32), in_=gate_d.transpose([1, 0, 2])),
        inc=LD, amt=16)
    t_lam = S.emit("sync", lambda e: e.dma_start(out=lam_sb[:], in_=lam_d), inc=LD, amt=16)
    t_sub = S.emit("sync", lambda e: e.dma_start(
        out=subln_c[:], in_=subln_d.rearrange("(p o) -> p o", o=1)), inc=LD, amt=16)
    S.emit("vector", lambda e: e.memset(eps_col[:], EPS))
    S.emit("vector", lambda e: e.memset(ones_bf[:], 1.0))
    S.emit("vector", lambda e: e.memset(ones_m1[:], 1.0))
    S.emit("vector", lambda e: e.memset(ones33a[:], 0.0))
    S.emit("vector", lambda e: e.memset(ones33b[:], 0.0))
    S.emit("vector", lambda e: e.memset(ones33a[:, 0:1], 1.0))
    t_ones = S.emit("vector", lambda e: e.memset(ones33b[:, 32:33], 1.0), inc=DV)
    tmp64 = sb("tmp64", [1, 64], F32)
    tl = S.emit("vector", lambda e: e.tensor_tensor(tmp64[:], lam_sb[0:1, 0:64],
                                                    lam_sb[0:1, 64:128], ALU.mult),
                wait=[t_lam], inc=DV)
    tl = S.emit("vector", lambda e: e.reduce_sum(lrow[0:1, 0:1], tmp64[:], axis=AX),
                wait=[tl], inc=DV)
    tl2 = S.emit("vector", lambda e: e.tensor_tensor(tmp64[:], lam_sb[0:1, 128:192],
                                                     lam_sb[0:1, 192:256], ALU.mult),
                 wait=[tl], inc=DV)
    tl2 = S.emit("vector", lambda e: e.reduce_sum(lrow[0:1, 1:2], tmp64[:], axis=AX),
                 wait=[tl2], inc=DV)
    ta = S.emit("scalar", lambda e: e.activation(lrow[0:1, 2:3], lrow[0:1, 0:1], AF.Exp),
                wait=[tl2], inc=AC)
    ta = S.emit("scalar", lambda e: e.activation(lrow[0:1, 3:4], lrow[0:1, 1:2], AF.Exp), inc=AC)
    tf_ = S.emit("vector", lambda e: e.tensor_tensor(lrow[0:1, 4:5], lrow[0:1, 3:4],
                                                     lrow[0:1, 2:3], ALU.subtract),
                 wait=[ta], inc=DV)
    t_lamf = S.emit("vector", lambda e: e.tensor_scalar(lrow[0:1, 4:5], lrow[0:1, 4:5],
                                                        -LAMBDA_INIT, None, ALU.add),
                    wait=[tf_], inc=DV)
    t_sube = S.emit("vector", lambda e: e.tensor_scalar(subln_c[:], subln_c[:],
                                                        1.0 - LAMBDA_INIT, None, ALU.mult),
                    wait=[t_sub], inc=DV)

    grpA = ExitStack()
    stats_ = grpA.enter_context(nc.sbuf_tensor("statsA", [128, 16], F32))
    xa = grpA.enter_context(nc.sbuf_tensor("xaA", [128, D], F32))
    xb = grpA.enter_context(nc.sbuf_tensor("xbA", [128, D], F32))
    lnbf_a = grpA.enter_context(nc.sbuf_tensor("lnbfA_a", [128, D], BF16))
    lnbf_b = grpA.enter_context(nc.sbuf_tensor("lnbfA_b", [128, D], BF16))
    psA = ExitStack()
    trp = [psA.enter_context(nc.psum_tensor(f"trpA{i}", [128, 1024], BF16)) for i in range(2)]
    prev_x_cons = [None, None]
    prev_ln_cons = [None, None]
    prev_tr = [None, None]
    prev_sq_read = [None, None]
    prev_rstd_read = [None, None]
    prev_lnb_rdA = [None, None]
    lnA_last = None
    for tt in range(NT):
        par = tt % 2
        stats = stats_[:, par * 8:(par + 1) * 8]
        xt = xa if par == 0 else xb
        lnb = lnbf_a if par == 0 else lnbf_b
        tld = S.emit("sync", lambda e, xt=xt, tt=tt: e.dma_start(
            out=xt[:], in_=x0_d[tt * 128:(tt + 1) * 128, :]),
            wait=[prev_x_cons[par]], inc=LD, amt=16)
        ts1 = S.emit("vector", lambda e, xt=xt, stats=stats: e.reduce_sum(
            stats[:, 0:1], xt[:], axis=AX),
            wait=[tld], inc=DV)
        ts2 = S.emit("scalar", lambda e, xt=xt, lnb=lnb, stats=stats: e.activation(
            lnb[:], xt[:], AF.Square, accum_out=stats[:, 1:2]),
            wait=[tld, prev_sq_read[par], prev_lnb_rdA[par]], inc=AC)
        tm = S.emit("vector", lambda e, stats=stats: e.tensor_scalar(
            stats[:, 2:3], stats[:, 0:1], 1.0 / D, None, ALU.mult),
            wait=[ts1, ts2], inc=DV)
        tm = S.emit("vector", lambda e, stats=stats: e.tensor_tensor(
            stats[:, 3:4], stats[:, 2:3], stats[:, 2:3], ALU.mult), wait=[tm], inc=DV)
        tm = S.emit("vector", lambda e, stats=stats: e.scalar_tensor_tensor(
            stats[:, 4:5], stats[:, 1:2], 1.0 / D, stats[:, 3:4], ALU.mult, ALU.subtract),
            wait=[tm], inc=DV)
        prev_sq_read[par] = tm
        tv = S.emit("scalar", lambda e, stats=stats: e.activation(
            stats[:, 5:6], stats[:, 4:5], AF.Ln, bias=eps_col[:]),
            wait=[tm, prev_rstd_read[par]], inc=AC)
        tv = S.emit("scalar", lambda e, stats=stats: e.activation(
            stats[:, 6:7], stats[:, 5:6], AF.Exp, scale=-0.5), inc=AC)
        tn = S.emit("vector", lambda e, xt=xt, lnb=lnb, stats=stats: e.tensor_scalar(
            lnb[:], xt[:], stats[:, 2:3], stats[:, 6:7], ALU.subtract, ALU.mult),
            wait=[tv, prev_ln_cons[par]], inc=DV)
        prev_rstd_read[par] = tn
        prev_x_cons[par] = tn
        packs = []
        for half in range(2):
            pr = trp[half]
            def tr_pack(e, lnb=lnb, pr=pr, half=half):
                ins = None
                for k in range(8):
                    j = half * 8 + k
                    ins = e.transpose(pr[:, k * 128:(k + 1) * 128],
                                      lnb[:, j * 128:(j + 1) * 128], eye[:])
                return ins
            tp = S.emit("tensor", tr_pack, wait=[tn, t_eye, prev_tr[half]], inc=PE)
            if half == 1:
                prev_lnb_rdA[par] = tp
            def ev(e, pr=pr, half=half, tt=tt):
                dst = lnT[:].rearrange("p (j t) -> p j t", t=T)[
                    :, half * 8:half * 8 + 8, tt * 128:(tt + 1) * 128]
                src = pr[:].rearrange("p (k c) -> p k c", c=128)
                return e.tensor_copy(dst, src)
            te = S.emit("vector", ev, wait=[tp], inc=DV)
            prev_tr[half] = te
            packs.append(te)
        prev_ln_cons[par] = packs[1]
        lnA_last = packs[1]
    t_lnA_done = lnA_last
    if debug:
        S.emit("gpsimd", lambda e: e.dma_start(out=dbg["lnT"], in_=lnT[:]),
               wait=[t_lnA_done], inc=ST, amt=16)

    psA.close()
    grpA.close()
    grpB = ExitStack()
    v_sb = grpB.enter_context(nc.sbuf_tensor("v_sb", [128, NT * D], BF16))
    wsl = [grpB.enter_context(nc.sbuf_tensor(f"wslB{i}", [128, D], BF16)) for i in range(2)]
    otile = [grpB.enter_context(nc.sbuf_tensor(f"otB{i}", [128, D], BF16)) for i in range(2)]
    psB = ExitStack()
    qp0 = psB.enter_context(nc.psum_tensor("qpB0", [128, D], F32))
    trB = [psB.enter_context(nc.psum_tensor(f"trB{i}", [128, 1024], BF16)) for i in range(2)]

    def qkv_proj(w_dram, which, prev_state):
        prev_w, prev_p, prev_o, prev_trB = prev_state
        last = None
        for i in range(NJ):
            par = i % 2
            tw = S.emit("gpsimd", lambda e, i=i, par=par, w_dram=w_dram: e.dma_start(
                out=wsl[par][:], in_=w_dram[i]),
                wait=[prev_w[par], t_lnA_done if prev_w[par] is None else None], inc=LW, amt=16)
            def mms(e, par=par):
                ins = None
                for j in range(NJ):
                    st = wsl[par][:, j * 128:(j + 1) * 128]
                    for c in range(4):
                        ins = e.matmul(qp0[:, c * 512:(c + 1) * 512], st,
                                       lnT[:, j * T + c * 512:j * T + (c + 1) * 512],
                                       start=(j == 0), stop=(j == NJ - 1),
                                       skip_group_check=True)
                return ins
            tp = S.emit("tensor", mms, wait=[tw, t_lnA_done, prev_p[0]], inc=PE)
            prev_w[par] = tp
            tev = S.emit("vector", lambda e, par=par: e.tensor_copy(otile[par][:], qp0[:]),
                         wait=[tp, prev_o[par]], inc=DV)
            prev_p[0] = tev
            if which < 2:
                tsp = S.emit("sync", lambda e, i=i, par=par, which=which: e.dma_start(
                    out=qkT_d[which, i], in_=otile[par][:]), wait=[tev], inc=ST, amt=16)
                prev_o[par] = tsp
                last = tsp
            else:
                toks = []
                for half in range(2):
                    pr = trB[half]
                    def trp_(e, par=par, pr=pr, half=half):
                        ins = None
                        for k in range(8):
                            s = half * 8 + k
                            ins = e.transpose(pr[:, k * 128:(k + 1) * 128],
                                              otile[par][:, s * 128:(s + 1) * 128], eye[:])
                        return ins
                    tpp = S.emit("tensor", trp_, wait=[tev, prev_trB[half]], inc=PE)
                    def evv(e, pr=pr, half=half, i=i):
                        dst = v_sb[:].rearrange("p (s c) -> p s c", c=D)[
                            :, half * 8:half * 8 + 8, i * 128:(i + 1) * 128]
                        src = pr[:].rearrange("p (k c) -> p k c", c=128)
                        return e.tensor_copy(dst, src)
                    tvv = S.emit("vector", evv, wait=[tpp], inc=DV)
                    prev_trB[half] = tvv
                    toks.append(tvv)
                prev_o[par] = toks[1]
                last = toks[1]
        return last

    stB = [[None, None], [None, None], [None, None], [None, None]]
    t_q = qkv_proj(wq_d, 0, stB)
    t_k = qkv_proj(wk_d, 1, stB)
    t_v = qkv_proj(wv_d, 2, stB)
    if debug:
        S.emit("gpsimd", lambda e: e.dma_start(out=dbg["v"], in_=v_sb[:]),
               wait=[t_v], inc=ST, amt=16)
        S.emit("gpsimd", lambda e: e.dma_start(out=dbg["qkT"], in_=qkT_d),
               wait=[t_q, t_k], inc=ST, amt=16)
    psB.close()

    qh = [grpB.enter_context(nc.sbuf_tensor(f"qh{i}", [128, T], BF16)) for i in range(2)]
    kh = [grpB.enter_context(nc.sbuf_tensor(f"kh{i}", [128, T], BF16)) for i in range(2)]
    probs = [grpB.enter_context(nc.sbuf_tensor(f"pr{i}", [128, 1024], BF16)) for i in range(2)]
    A_sb = [grpB.enter_context(nc.sbuf_tensor(f"Asb{i}", [128, 512], BF16)) for i in range(2)]
    rrow_a = grpB.enter_context(nc.sbuf_tensor("rrow_a", [1, 512], BF16))
    rrow_b = grpB.enter_context(nc.sbuf_tensor("rrow_b", [1, 512], BF16))
    t1row = grpB.enter_context(nc.sbuf_tensor("t1row", [1, 512], F32))
    t2row = grpB.enter_context(nc.sbuf_tensor("t2row", [1, 512], F32))
    comb1 = grpB.enter_context(nc.sbuf_tensor("comb1", [128, 512], F32))
    comb2 = grpB.enter_context(nc.sbuf_tensor("comb2", [128, 512], BF16))
    sqc = grpB.enter_context(nc.sbuf_tensor("sqc", [128, 512], BF16))
    rmsrow = grpB.enter_context(nc.sbuf_tensor("rmsrow", [1, 512], F32))
    attn_out = [grpB.enter_context(nc.sbuf_tensor(f"attn_out{i}", [128, 512], BF16))
                for i in range(2)]
    lnlam = grpB.enter_context(nc.sbuf_tensor("lnlam", [1, 1], F32))

    t_lnl = S.emit("vector", lambda e: e.tensor_scalar(lnlam[:], lrow[0:1, 4:5], -1.0, None,
                                                       ALU.mult), wait=[t_lamf], inc=DV)
    t_lnl = S.emit("scalar", lambda e: e.activation(lnlam[:], lnlam[:], AF.Ln), wait=[t_lnl], inc=AC)

    psC = ExitStack()
    SpA = psC.enter_context(nc.psum_tensor("SpA", [128, 1024], F32))
    SpB = psC.enter_context(nc.psum_tensor("SpB", [128, 1024], F32))
    A1p = psC.enter_context(nc.psum_tensor("A1p", [128, 512], F32))
    A2p = psC.enter_context(nc.psum_tensor("A2p", [128, 512], F32))
    SUMp = psC.enter_context(nc.psum_tensor("SUMp", [33, 512], F32))
    EPIp = psC.enter_context(nc.psum_tensor("EPIp", [128, 512], F32))

    qk_scale = float(HD) ** -0.5
    prev_qh = [None, None]
    prev_kh = [None, None]
    last_S_of_head = None
    PH = None
    P2H = [None, None]
    epi_sched = []
    t_C_done_hold = [None]

    for h in range(NH):
        hp = h % 2
        tqh = S.emit("sync", lambda e, h=h, hp=hp: e.dma_start(out=qh[hp][:], in_=qkT_d[0, h]),
                     wait=[prev_qh[hp], t_q if h < 2 else None], inc=LD, amt=16)
        tkh = S.emit("sync", lambda e, h=h, hp=hp: e.dma_start(out=kh[hp][:], in_=qkT_d[1, h]),
                     wait=[prev_kh[hp], t_k if h < 2 else None], inc=LD, amt=16)
        for c in range(4):
            t0 = c * 512
            first = (h == 0 and c == 0)
            op = (h * 4 + c) % 2
            cur_exp = [None] * 16
            cur_pv = [None] * 16
            pmap = {}
            for slot, fn in epi_sched:
                pmap.setdefault(slot, []).append(fn)
            epi_sched = []
            for st in range(17):
                if st < 16:
                    par = st % 2
                    Sp = SpA if par == 0 else SpB
                    if st >= 2:
                        w_bank = cur_exp[st - 2]
                    else:
                        w_bank = PH["exp"][14 + st] if PH is not None else None
                    def smm(e, hp=hp, st=st, t0=t0, Sp=Sp):
                        e.matmul(Sp[:, 0:512], kh[hp][0:64, st * 128:(st + 1) * 128],
                                 qh[hp][0:64, t0:t0 + 512], start=True, stop=True,
                                 skip_group_check=True)
                        return e.matmul(Sp[:, 512:1024], kh[hp][64:128, st * 128:(st + 1) * 128],
                                        qh[hp][64:128, t0:t0 + 512], start=True, stop=True,
                                        skip_group_check=True)
                    tS = S.emit("tensor", smm, wait=[tqh, tkh, w_bank], inc=PE)
                    last_S_of_head = tS
                if st >= 1:
                    sv = st - 1
                    parv = sv % 2
                    if sv == 0:
                        def pvS(e, pb=probs[parv]):
                            e.matmul(SUMp[:], ones33a[:], pb[:, 0:512],
                                     start=True, stop=False, skip_group_check=True)
                            return e.matmul(SUMp[:], ones33b[:], pb[:, 512:1024],
                                            start=False, stop=False, skip_group_check=True)
                        wS = [cur_exp[0]]
                        if PH is not None:
                            wS.append(lambda H=PH: H["lg2"][0])
                        if first:
                            wS += [t_ones, t_v]
                        S.emit("tensor", pvS, wait=wS, inc=PE)
                        def pvA(e, h=h, pb=probs[parv]):
                            v_st = v_sb[:, h * 128:(h + 1) * 128]
                            e.matmul(A1p[:], v_st, pb[:, 0:512],
                                     start=True, stop=False, skip_group_check=True)
                            return e.matmul(A2p[:], v_st, pb[:, 512:1024],
                                            start=True, stop=False, skip_group_check=True)
                        wA = []
                        if PH is not None:
                            wA.append(lambda H=PH: H["ev2"][0])
                        tP = S.emit("tensor", pvA, wait=wA, inc=PE)
                    else:
                        def pv(e, h=h, sv=sv, pb=probs[parv]):
                            v_st = v_sb[:, sv * D + h * 128:sv * D + (h + 1) * 128]
                            e.matmul(A1p[:], v_st, pb[:, 0:512],
                                     start=False, stop=(sv == 15), skip_group_check=True)
                            e.matmul(A2p[:], v_st, pb[:, 512:1024],
                                     start=False, stop=(sv == 15), skip_group_check=True)
                            e.matmul(SUMp[:], ones33a[:], pb[:, 0:512],
                                     start=False, stop=False, skip_group_check=True)
                            return e.matmul(SUMp[:], ones33b[:], pb[:, 512:1024],
                                            start=False, stop=(sv == 15), skip_group_check=True)
                        tP = S.emit("tensor", pv, wait=[cur_exp[sv]], inc=PE)
                    cur_pv[sv] = tP
                if st < 16:
                    par = st % 2
                    Sp = SpA if par == 0 else SpB
                    if st >= 2:
                        w_pb = cur_pv[st - 2]
                    elif PH is not None:
                        w_pb = PH["pv14"] if st == 0 else PH["pv15"]
                    else:
                        w_pb = None
                    tE = S.emit("scalar", lambda e, Sp=Sp, pb=probs[par]: e.activation(
                        pb[:], Sp[:], AF.Exp, scale=qk_scale),
                        wait=[w_pb, last_S_of_head], inc=AC)
                    cur_exp[st] = tE
                for fn in pmap.get(st, []):
                    fn()
            H = {k: [None] for k in ("ev1", "ev2", "lg1", "lg2", "ra", "rb", "b1", "c1",
                                     "b2", "c2", "cb", "sq", "qq", "ms", "mc", "br",
                                     "fin", "sp")}
            H["exp"] = cur_exp
            H["pv14"] = cur_pv[14]
            H["pv15"] = cur_pv[15]
            t_pv15 = cur_pv[15]
            e_first = first
            OP = PH
            SPH = P2H[op]

            def reg(slot, eng, fn, waits, hkey, incq, amt=1, H=H):
                hold = H[hkey]
                def run():
                    rw = [(w() if callable(w) else w) for w in waits]
                    hold[0] = S.emit(eng, fn, wait=rw, inc=incq, amt=amt)
                epi_sched.append((slot, run))

            reg(0, "vector", lambda e: e.tensor_copy(A_sb[0][:], A1p[:]),
                [t_pv15], "ev1", DV)
            reg(0, "vector", lambda e: e.tensor_copy(A_sb[1][:], A2p[:]),
                [], "ev2", DV)
            reg(0, "scalar", lambda e: e.activation(t1row[:], SUMp[0:1, :], AF.Ln),
                [t_pv15], "lg1", AC)
            reg(0, "scalar", lambda e: e.activation(t2row[:], SUMp[32:33, :], AF.Ln),
                [], "lg2", AC)
            reg(1, "scalar", lambda e: e.activation(rrow_a[:], t1row[:], AF.Exp, scale=-1.0),
                [(lambda H=OP: H["br"][0]) if OP is not None else None], "ra", AC)
            reg(1, "scalar", lambda e: e.activation(rrow_b[:], t2row[:], AF.Exp, scale=-1.0,
                                                    bias=lnlam[:]),
                [(lambda H=OP: H["b2"][0]) if OP is not None else None,
                 t_lnl if e_first else None], "rb", AC)
            reg(3, "tensor", lambda e: e.matmul(EPIp[:], ones_bf[:], rrow_a[:], start=True,
                                                stop=True, skip_group_check=True),
                [lambda H=H: H["ra"][0],
                 (lambda H=OP: H["fin"][0]) if OP is not None else None], "b1", PE)
            reg(4, "vector", lambda e: e.tensor_tensor(comb1[:], A_sb[0][:], EPIp[:], ALU.mult),
                [lambda H=H: H["b1"][0]], "c1", DV)
            reg(5, "tensor", lambda e: e.matmul(EPIp[:], ones_bf[:], rrow_b[:], start=True,
                                                stop=True, skip_group_check=True),
                [lambda H=H: H["rb"][0], lambda H=H: H["c1"][0]], "b2", PE)
            reg(6, "vector", lambda e: e.tensor_tensor(comb2[:], A_sb[1][:], EPIp[:], ALU.mult),
                [lambda H=H: H["b2"][0]], "c2", DV)
            reg(6, "vector", lambda e: e.tensor_tensor(comb1[:], comb1[:], comb2[:],
                                                       ALU.subtract),
                [lambda H=H: H["c2"][0]], "cb", DV)
            reg(7, "scalar", lambda e: e.activation(sqc[:], comb1[:], AF.Square),
                [lambda H=H: H["cb"][0],
                 (lambda H=OP: H["qq"][0]) if OP is not None else None], "sq", AC)
            reg(8, "tensor", lambda e: e.matmul(EPIp[0:1, :], ones_m1[:], sqc[:], start=True,
                                                stop=True, skip_group_check=True),
                [lambda H=H: H["sq"][0], lambda H=H: H["c2"][0]], "qq", PE)
            reg(9, "vector", lambda e: e.tensor_scalar(rmsrow[:], EPIp[0:1, :], 1.0 / 128,
                                                       None, ALU.mult),
                [lambda H=H: H["qq"][0]], "ms", DV)
            reg(9, "scalar", lambda e: e.activation(rmsrow[:], rmsrow[:], AF.Ln,
                                                    bias=eps_col[0:1, :]),
                [lambda H=H: H["ms"][0]], "ms", AC)
            reg(9, "scalar", lambda e: e.activation(rmsrow[:], rmsrow[:], AF.Exp, scale=-0.5),
                [], "ms", AC)
            reg(10, "vector", lambda e: e.tensor_copy(rrow_a[:], rmsrow[:]),
                [lambda H=H: H["ms"][0], lambda H=H: H["b1"][0]], "mc", DV)
            reg(12, "tensor", lambda e: e.matmul(EPIp[:], ones_bf[:], rrow_a[:], start=True,
                                                 stop=True, skip_group_check=True),
                [lambda H=H: H["mc"][0], lambda H=H: H["c1"][0]], "br", PE)
            ao = attn_out[op]
            reg(13, "vector", lambda e, ao=ao: e.scalar_tensor_tensor(
                ao[:], EPIp[:], subln_c[:], comb1[:], ALU.mult, ALU.mult),
                [lambda H=H: H["br"][0], t_sube if e_first else None,
                 (lambda H=SPH: H["sp"][0]) if SPH is not None else None], "fin", DV)

            def f_spill(e, h=h, c=c, ao=ao):
                dst = attnT_d[c * 4:(c + 1) * 4, :, h, :].transpose([1, 0, 2])
                srcp = ao[:].rearrange("p (k t) -> p k t", t=128)
                return e.dma_start(out=dst, in_=srcp)
            reg(14, "sync", f_spill, [lambda H=H: H["fin"][0]], "sp", ST, amt=16)
            t_C_done_hold[0] = H["sp"]

            PH = H
            P2H[op] = H
        prev_qh[hp] = last_S_of_head
        prev_kh[hp] = last_S_of_head
    for slot, fn in sorted(epi_sched, key=lambda x: x[0]):
        fn()
    epi_sched = []
    t_C_done = t_C_done_hold[0][0]
    assert t_C_done is not None
    psC.close()

    grpB.close()
    grpD = ExitStack()
    wo_sb = grpD.enter_context(nc.sbuf_tensor("wo_sb", [128, NJ * D], BF16))
    att_in = [grpD.enter_context(nc.sbuf_tensor(f"attin{i}", [128, D], BF16)) for i in range(2)]
    x1sb = [grpD.enter_context(nc.sbuf_tensor(f"x1sb{i}", [128, D], F32)) for i in range(2)]
    ln2f = [grpD.enter_context(nc.sbuf_tensor(f"ln2f{i}", [128, D], F32)) for i in range(2)]
    ln2Tf = grpD.enter_context(nc.sbuf_tensor("ln2Tf", [128, 1024], F32))
    logit_sb = grpD.enter_context(nc.sbuf_tensor("logit_sb", [128, 32], F32))
    rw_sb = grpD.enter_context(nc.sbuf_tensor("rw_sb", [128, 32], F32))
    rmask = grpD.enter_context(nc.sbuf_tensor("rmask", [128, 32], F32))
    rtmp = grpD.enter_context(nc.sbuf_tensor("rtmp", [128, 32], F32))
    rcol = grpD.enter_context(nc.sbuf_tensor("rcol", [128, 4], F32))
    statsD_ = grpD.enter_context(nc.sbuf_tensor("statsD", [128, 16], F32))
    xa = grpD.enter_context(nc.sbuf_tensor("xaD", [128, D], F32))
    xb = grpD.enter_context(nc.sbuf_tensor("xbD", [128, D], F32))
    lnbf_a = grpD.enter_context(nc.sbuf_tensor("lnbfD_a", [128, D], BF16))
    lnbf_b = grpD.enter_context(nc.sbuf_tensor("lnbfD_b", [128, D], BF16))

    two_pref = []
    tw_prev = t_C_done
    for j in range(NJ):
        tw = S.emit("gpsimd", lambda e, j=j: e.dma_start(
            out=wo_sb[:, j * D:(j + 1) * D], in_=wo_d[j]),
            wait=[tw_prev if j == 0 else None], inc=LW, amt=16)
        two_pref.append(tw)

    psD = ExitStack()
    wopsH = [psD.enter_context(nc.psum_tensor(f"wopsH{i}", [128, 1024], F32)) for i in range(2)]
    trD = psD.enter_context(nc.psum_tensor("trD", [128, 2048], BF16))
    trDf = psD.enter_context(nc.psum_tensor("trDf", [128, 512], F32))
    logp = psD.enter_context(nc.psum_tensor("logp", [128, 32], F32))

    prev_att = [None, None]
    prev_x0_cons = [None, None]
    prev_x1_cons = [None, None]
    prev_x1_dma = [None, None]
    prev_wops = [None, None]
    prev_trD = [None, None]
    prev_trDf = None
    prev_routmm = None
    prev_cw = None
    prev_sq_read = [None, None]
    prev_rstd_read = [None, None]
    prev_ln2f_cons = [None, None]
    prev_lnb_rd = [None, None]
    t_ln2_done = None
    t_router_pe_final = None
    stA = {}

    for tt in range(NT + 1):
        par = tt % 2
        if tt < NT:
            tat = S.emit("sync", lambda e, tt=tt, par=par: e.dma_start(
                out=att_in[par][:].rearrange("p (ch t) -> p ch t", t=128), in_=attnT_d[tt]),
                wait=[prev_att[par], t_C_done], inc=LD, amt=16)
            xt = xa if par == 0 else xb
            tx0 = S.emit("sync", lambda e, tt=tt, xt=xt: e.dma_start(
                out=xt[:], in_=x0_d[tt * 128:(tt + 1) * 128, :]),
                wait=[prev_x0_cons[par], t_C_done if tt < 2 else None], inc=LD, amt=16)
            x1t = x1sb[par]
            lnb = lnbf_a if par == 0 else lnbf_b
            tx1h = [None, None]
            for hw_ in range(2):
                def wom(e, par=par, hw_=hw_):
                    ins = None
                    for cc in range(NJ):
                        st = att_in[par][:, cc * 128:(cc + 1) * 128]
                        for c in range(2):
                            cg = hw_ * 2 + c
                            ins = e.matmul(wopsH[hw_][:, c * 512:(c + 1) * 512], st,
                                           wo_sb[:, cc * D + cg * 512:cc * D + (cg + 1) * 512],
                                           start=(cc == 0), stop=(cc == NJ - 1),
                                           skip_group_check=True)
                    return ins
                tp = S.emit("tensor", wom, wait=[tat, two_pref[-1], prev_wops[hw_]], inc=PE)
                if hw_ == 1:
                    prev_att[par] = tp
                tx1 = S.emit("vector", lambda e, xt=xt, x1t=x1t, hw_=hw_: e.tensor_tensor(
                    x1t[:, hw_ * 1024:(hw_ + 1) * 1024], xt[:, hw_ * 1024:(hw_ + 1) * 1024],
                    wopsH[hw_][:], ALU.add),
                    wait=[tp, tx0 if hw_ == 0 else None,
                          prev_x1_cons[par] if hw_ == 0 else None,
                          prev_x1_dma[par] if hw_ == 0 else None], inc=DV)
                prev_wops[hw_] = tx1
                tx1h[hw_] = tx1
            tx1 = tx1h[1]
            prev_x0_cons[par] = tx1
            tsx = S.emit("sync", lambda e, x1t=x1t, tt=tt: e.dma_start(
                out=x1_d[tt * 128:(tt + 1) * 128, :], in_=x1t[:]), wait=[tx1], inc=ST, amt=16)
            prev_x1_dma[par] = tsx
            stats = statsD_[:, par * 8:(par + 1) * 8]
            ts1 = S.emit("vector", lambda e, x1t=x1t, stats=stats: e.reduce_sum(
                stats[:, 0:1], x1t[:], axis=AX),
                wait=[tx1], inc=DV)
            ts2 = S.emit("scalar", lambda e, x1t=x1t, lnb=lnb, stats=stats: e.activation(
                lnb[:], x1t[:], AF.Square, accum_out=stats[:, 1:2]),
                wait=[tx1, prev_sq_read[par], prev_lnb_rd[par]], inc=AC)
            tm = S.emit("vector", lambda e, stats=stats: e.tensor_scalar(
                stats[:, 2:3], stats[:, 0:1], 1.0 / D, None, ALU.mult),
                wait=[ts1, ts2], inc=DV)
            tm = S.emit("vector", lambda e, stats=stats: e.tensor_tensor(
                stats[:, 3:4], stats[:, 2:3], stats[:, 2:3], ALU.mult), wait=[tm], inc=DV)
            tm = S.emit("vector", lambda e, stats=stats: e.scalar_tensor_tensor(
                stats[:, 4:5], stats[:, 1:2], 1.0 / D, stats[:, 3:4], ALU.mult, ALU.subtract),
                wait=[tm], inc=DV)
            prev_sq_read[par] = tm
            tv = S.emit("scalar", lambda e, stats=stats: e.activation(
                stats[:, 5:6], stats[:, 4:5], AF.Ln, bias=eps_col[:]),
                wait=[tm, prev_rstd_read[par]], inc=AC)
            tv = S.emit("scalar", lambda e, stats=stats: e.activation(
                stats[:, 6:7], stats[:, 5:6], AF.Exp, scale=-0.5), inc=AC)
            l2f = ln2f[par]
            tfl = S.emit("vector", lambda e, x1t=x1t, l2f=l2f, stats=stats: e.tensor_scalar(
                l2f[:], x1t[:], stats[:, 2:3], stats[:, 6:7], ALU.subtract, ALU.mult),
                wait=[tv, prev_ln2f_cons[par]], inc=DV)
            prev_rstd_read[par] = tfl
            prev_x1_cons[par] = tfl
            tn = S.emit("vector", lambda e, lnb=lnb, l2f=l2f: e.tensor_copy(lnb[:], l2f[:]),
                        wait=[tfl], inc=DV)
            stA[tt] = (tfl, tn, lnb, l2f)
        if tt >= 1:
            tb = tt - 1
            parb = tb % 2
            tfl, tn, lnb, l2f = stA.pop(tb)
            packs = []
            for half in range(2):
                def tr_pack(e, lnb=lnb, half=half):
                    ins = None
                    for k in range(8):
                        j = half * 8 + k
                        ins = e.transpose(trD[:, half * 1024 + k * 128:half * 1024 + (k + 1) * 128],
                                          lnb[:, j * 128:(j + 1) * 128], eye[:])
                    return ins
                tpd = S.emit("tensor", tr_pack, wait=[tn, prev_trD[half],
                                                      t_eye if tb == 0 and half == 0 else None],
                             inc=PE)
                def ev(e, half=half, tb=tb):
                    dst = lnT[:].rearrange("p (j t) -> p j t", t=T)[
                        :, half * 8:half * 8 + 8, tb * 128:(tb + 1) * 128]
                    src = trD[:, half * 1024:(half + 1) * 1024].rearrange(
                        "p (k c) -> p k c", c=128)
                    return e.tensor_copy(dst, src)
                te = S.emit("vector", ev, wait=[tpd], inc=DV)
                prev_trD[half] = te
                if half == 1:
                    prev_lnb_rd[parb] = tpd
                packs.append(te)
            t_ln2_done = packs[1]
            rmmA = None
            trt = None
            for q4 in range(4):
                def trf(e, q4=q4, l2f=l2f):
                    ins = None
                    for k in range(4):
                        j = q4 * 4 + k
                        ins = e.transpose(trDf[:, k * 128:(k + 1) * 128],
                                          l2f[:, j * 128:(j + 1) * 128], eyef[:])
                    return ins
                tpf = S.emit("tensor", trf, wait=[tfl, t_eyef if (tb == 0 and q4 == 0) else None,
                                                  prev_trDf], inc=PE)
                tef = S.emit("vector", lambda e, q4=q4: e.tensor_copy(
                    ln2Tf[:, (q4 % 2) * 512:(q4 % 2 + 1) * 512], trDf[:]),
                    wait=[tpf, prev_routmm if q4 < 2 else rmmA], inc=DV)
                prev_trDf = tef
                if q4 % 2 == 1:
                    def routmm(e, q4=q4):
                        ins = None
                        for jj in range(8):
                            j = (q4 // 2) * 8 + jj
                            ins = e.matmul(logp[:], ln2Tf[:, jj * 128:(jj + 1) * 128],
                                           gate_sb[:, j * 32:(j + 1) * 32],
                                           start=(j == 0), stop=(j == NJ - 1),
                                           skip_group_check=True)
                        return ins
                    trt = S.emit("tensor", routmm,
                                 wait=[tef, t_gate if tb == 0 and q4 == 1 else None,
                                       prev_routmm if q4 == 1 else None], inc=PE)
                    if q4 == 1:
                        rmmA = trt
                if q4 == 3:
                    prev_ln2f_cons[parb] = tef
            t_router_pe_final = trt
            tlg = S.emit("vector", lambda e: e.tensor_copy(logit_sb[:], logp[:]), wait=[trt], inc=DV)
            prev_routmm = tlg
            trw = S.emit("scalar", lambda e: e.activation(rw_sb[:], logit_sb[:], AF.Exp),
                         wait=[tlg], inc=AC)
            tmk = S.emit("vector", lambda e: e.tensor_copy(rtmp[:], logit_sb[:]), wait=[tlg], inc=DV)
            for k in range(3):
                tmk = S.emit("vector", lambda e, k=k: e.reduce_max(rcol[:, k:k + 1], rtmp[:], axis=AX),
                             wait=[tmk], inc=DV)
                tmk = S.emit("vector", lambda e, k=k: e.tensor_scalar(
                    rmask[:], rtmp[:], rcol[:, k:k + 1], None, ALU.is_ge), wait=[tmk], inc=DV)
                tmk = S.emit("vector", lambda e: e.scalar_tensor_tensor(
                    rtmp[:], rmask[:], -1e30, rtmp[:], ALU.mult, ALU.add), wait=[tmk], inc=DV)
            tmk = S.emit("vector", lambda e: e.reduce_max(rcol[:, 3:4], rtmp[:], axis=AX),
                         wait=[tmk], inc=DV)
            tmk = S.emit("vector", lambda e: e.tensor_scalar(
                rmask[:], logit_sb[:], rcol[:, 3:4], None, ALU.is_ge), wait=[tmk], inc=DV)
            tcw = S.emit("vector", lambda e, tb=tb: e.tensor_tensor(
                cw_sb[:, tb * 32:(tb + 1) * 32], rw_sb[:], rmask[:], ALU.mult),
                wait=[tmk, trw], inc=DV)
            tcw = S.emit("vector", lambda e, tb=tb: e.reduce_sum(
                rcol[:, 0:1], cw_sb[:, tb * 32:(tb + 1) * 32], axis=AX), wait=[tcw], inc=DV)
            tcw = S.emit("vector", lambda e: e.reciprocal(rcol[:, 1:2], rcol[:, 0:1]),
                         wait=[tcw], inc=DV)
            tcw = S.emit("vector", lambda e, tb=tb: e.tensor_scalar(
                cw_sb[:, tb * 32:(tb + 1) * 32], cw_sb[:, tb * 32:(tb + 1) * 32],
                rcol[:, 1:2], None, ALU.mult), wait=[tcw], inc=DV)
            def cws(e, tb=tb):
                v4 = cw_sb[:].rearrange("p (tq c g) -> p tq c g", c=16, g=2)
                return e.tensor_tensor(cws_sb[:, tb * 16:(tb + 1) * 16],
                                       v4[:, tb, :, 0], v4[:, tb, :, 1], ALU.add)
            tcs = S.emit("vector", cws, wait=[tcw], inc=DV)
            tcs = S.emit("vector", lambda e, tb=tb: e.tensor_scalar(
                cws_sb[:, tb * 16:(tb + 1) * 16], cws_sb[:, tb * 16:(tb + 1) * 16],
                0.05, None, ALU.mult), wait=[tcs], inc=DV)
            prev_cw = tcs
    psD.close()
    if debug:
        S.emit("gpsimd", lambda e: e.dma_start(out=dbg["ln2T"], in_=lnT[:]),
               wait=[t_ln2_done], inc=ST, amt=16)
        S.emit("sync", lambda e: e.dma_start(out=dbg["cw"], in_=cw_sb[:]),
               wait=[prev_cw], inc=ST, amt=16)
        S.emit("sync", lambda e: e.dma_start(out=dbg["cws"], in_=cws_sb[:]),
               wait=[prev_cw], inc=ST, amt=16)
        S.emit("gpsimd", lambda e: e.dma_start(out=dbg["attnT"], in_=attnT_d),
               wait=[t_C_done], inc=ST, amt=16)

    grpD.close()
    grpE = ExitStack()
    acc = grpE.enter_context(nc.sbuf_tensor("acc", [128, 8 * D], F32))
    a_sb = [grpE.enter_context(nc.sbuf_tensor(f"a_sb{i}", [128, 6 * 1024], BF16)) for i in range(2)]
    wg_s = [grpE.enter_context(nc.sbuf_tensor(f"wg_s{i}", [128, D], BF16)) for i in range(2)]
    wu_s = [grpE.enter_context(nc.sbuf_tensor(f"wu_s{i}", [128, D], BF16)) for i in range(2)]
    wd_s = [grpE.enter_context(nc.sbuf_tensor(f"wd_s{i}", [128, 2 * 6 * 512], BF16)) for i in range(2)]
    sg_sb = grpE.enter_context(nc.sbuf_tensor("sg_sb", [128, 1024], BF16))

    psE = ExitStack()
    pg = psE.enter_context(nc.psum_tensor("pgE", [128, 1024], F32))
    pu = psE.enter_context(nc.psum_tensor("puE", [128, 1024], F32))
    dn = [psE.enter_context(nc.psum_tensor(f"dn{i}", [128, 1024], F32)) for i in range(2)]

    t_out_final = []
    first_E = [True]
    prev_wg = [None, None]
    prev_wu = [None, None]
    prev_wd = [None, None]
    prev_a = [None, None]
    prev_mult = [None]
    prev_dn = [None, None]
    last_evac_h = [None]
    qctr = [0]
    pending = []
    prev_wd_emit = [None, None]

    def flush(n):
        for _ in range(n):
            if not pending:
                break
            fn, marker = pending.pop(0)
            fn()

    def fire_wd():
        for hp2 in range(2):
            if prev_wd_emit[hp2] is not None:
                prev_wd_emit[hp2]()
                prev_wd_emit[hp2] = None

    for half in range(2):
        thalf = half * 1024
        acc_init = []
        for k in range(8):
            tt = half * 8 + k
            ti = S.emit("sync", lambda e, k=k, tt=tt: e.dma_start(
                out=acc[:, k * D:(k + 1) * D], in_=x1_d[tt * 128:(tt + 1) * 128, :]),
                wait=[prev_cw if k == 0 else None,
                      t_out_final[-1] if (k == 0 and t_out_final) else None], inc=LD, amt=16)
            acc_init.append(ti)
        t_acc_ready = acc_init[-1]

        for ee in range(EL + ES):
            big = ee < EL
            epar = ee % 2
            n_i = 6 if big else 1
            ncols = 6 * 512 if big else 512
            nch = n_i
            npend = len(pending)
            t_a_parts = []
            my_twd = [None, None]
            def mk_wd(hp2, ee=ee, big=big, ncols=ncols, my_twd=my_twd):
                def emit_wd():
                    for hci in range(2):
                        hc = hp2 * 2 + hci
                        srcd = wd_d[ee, hc] if big else cwd_d[ee - EL, hc]
                        my_twd[hp2] = S.emit("gpsimd", lambda e, srcd=srcd, hp2=hp2,
                                             hci=hci, ncols=ncols: e.dma_start(
                            out=wd_s[hp2][:, hci * 3072:hci * 3072 + ncols], in_=srcd),
                            wait=[prev_wd[hp2] if hci == 0 else None], inc=LW, amt=16)
                return emit_wd
            my_wd_emit = [mk_wd(0), mk_wd(1)]
            chi = 0
            for ih in range(n_i):
                wpar = (ih % 2) if big else (ee % 2)
                srcg = wg_d[ee, ih] if big else cwg_d[ee - EL]
                srcu = wu_d[ee, ih] if big else cwu_d[ee - EL]
                first_wait = [t_router_pe_final, prev_cw] if first_E[0] else []
                twg = S.emit("gpsimd", lambda e, srcg=srcg, wpar=wpar: e.dma_start(
                    out=wg_s[wpar][:], in_=srcg),
                    wait=[prev_wg[wpar]] + first_wait, inc=LW, amt=16)
                twu = S.emit("gpsimd", lambda e, srcu=srcu, wpar=wpar: e.dma_start(
                    out=wu_s[wpar][:], in_=srcu), wait=[prev_wu[wpar]], inc=LW, amt=16)
                first_E[0] = False
                if ih == min(1, n_i - 1):
                    fire_wd()
                qctr[0] += 1
                def gu(e, wpar=wpar, thalf=thalf):
                    ins = None
                    for hh in range(NJ):
                        stg = wg_s[wpar][:, hh * 128:(hh + 1) * 128]
                        stu = wu_s[wpar][:, hh * 128:(hh + 1) * 128]
                        for cc in range(2):
                            mv = lnT[:, hh * T + thalf + cc * 512:hh * T + thalf + (cc + 1) * 512]
                            e.matmul(pg[:, cc * 512:(cc + 1) * 512], stg, mv,
                                     start=(hh == 0), stop=(hh == NJ - 1),
                                     skip_group_check=True)
                        for cc in range(2):
                            mv = lnT[:, hh * T + thalf + cc * 512:hh * T + thalf + (cc + 1) * 512]
                            ins = e.matmul(pu[:, cc * 512:(cc + 1) * 512], stu, mv,
                                           start=(hh == 0), stop=(hh == NJ - 1),
                                           skip_group_check=True)
                    return ins
                tgu = S.emit("tensor", gu, wait=[twg, twu, prev_mult[0],
                                                 t_ln2_done if qctr[0] == 1 and half == 0 else None],
                             inc=PE)
                tsl = S.emit("scalar", lambda e: e.activation(
                    sg_sb[:], pg[:], AF.Silu), wait=[tgu], inc=AC)
                tmu = S.emit("vector", lambda e, epar=epar, ih=ih: e.tensor_tensor(
                    a_sb[epar][:, ih * 1024:(ih + 1) * 1024], sg_sb[:], pu[:], ALU.mult),
                    wait=[tsl, prev_a[epar] if ih == 0 else None], inc=DV)
                prev_mult[0] = tmu
                t_a_parts.append(tmu)
                chi += 1
                if chi >= 2:
                    tgt = min(npend, ((chi - 1) * npend + nch - 2) // max(1, nch - 1))
                    flush(tgt - (npend - len(pending)))
                prev_wg[wpar] = tgu
                prev_wu[wpar] = tgu
            flush(len(pending))
            fire_wd()
            t_a = t_a_parts[-1]
            t_a_pre = t_a_parts[-2] if big else None
            cw_col = cw_sb if big else cws_sb
            cwi = ee if big else ee - EL
            ncol = 32 if big else 16
            newpend = []
            for hp2 in range(2):
                for k in range(8):
                    dpar = k % 2
                    split = big and k < 2
                    marker = hp2 if k == 7 else None
                    def unit(e=None, epar=epar, hp2=hp2, k=k, n_i=n_i, dpar=dpar, big=big,
                             t_a=t_a, t_a_pre=t_a_pre, my_twd=my_twd, split=split,
                             cw_col=cw_col, cwi=cwi, ncol=ncol, half=half,
                             ee=ee, t_acc_ready=t_acc_ready):
                        twd = my_twd[hp2]
                        def down(e, ih_lo=0, ih_hi=None):
                            if ih_hi is None:
                                ih_hi = n_i
                            ins = None
                            for ih in range(ih_lo, ih_hi):
                                st_a = a_sb[epar][:, ih * 1024 + k * 128:ih * 1024 + (k + 1) * 128]
                                e.matmul(dn[dpar][:, 0:512], st_a,
                                         wd_s[hp2][:, ih * 512:(ih + 1) * 512],
                                         start=(ih == 0), stop=(ih == n_i - 1),
                                         skip_group_check=True)
                                ins = e.matmul(dn[dpar][:, 512:1024], st_a,
                                               wd_s[hp2][:, 3072 + ih * 512:3072 + (ih + 1) * 512],
                                               start=(ih == 0), stop=(ih == n_i - 1),
                                               skip_group_check=True)
                            return ins
                        if split:
                            S.emit("tensor", lambda e: down(e, ih_lo=0, ih_hi=n_i - 1),
                                   wait=[t_a_pre, twd, prev_dn[dpar]], inc=PE)
                            tdn = S.emit("tensor", lambda e: down(e, ih_lo=n_i - 1),
                                         wait=[t_a], inc=PE)
                        else:
                            tdn = S.emit("tensor", down, wait=[t_a, twd, prev_dn[dpar]], inc=PE)
                        tev = S.emit("vector", lambda e: e.scalar_tensor_tensor(
                            acc[:, k * D + hp2 * 1024:k * D + (hp2 + 1) * 1024], dn[dpar][:],
                            cw_col[:, (half * 8 + k) * ncol + cwi:(half * 8 + k) * ncol + cwi + 1],
                            acc[:, k * D + hp2 * 1024:k * D + (hp2 + 1) * 1024],
                            ALU.mult, ALU.add),
                            wait=[tdn, t_acc_ready if (ee == 0 and hp2 == 0) else None], inc=DV)
                        prev_dn[dpar] = tev
                        last_evac_h[0] = tev
                        prev_wd[hp2] = tev
                        prev_a[epar] = tev
                    newpend.append((unit, marker))
            pending = newpend
            prev_wd_emit = my_wd_emit
        fire_wd()
        flush(len(pending))
        for k in range(8):
            tt = half * 8 + k
            to = S.emit("sync", lambda e, k=k, tt=tt: e.dma_start(
                out=out_d[tt * 128:(tt + 1) * 128, :], in_=acc[:, k * D:(k + 1) * D]),
                wait=[last_evac_h[0] if k == 0 else None], inc=ST, amt=16)
            t_out_final.append(to)
    psE.close()

    S.emit("sync", lambda e: e.wait_ge(ST.sem, ST.n))

    with nc.Block() as block:
        S.run_all(block)
    grpE.close()
    ctx.close()
    return nc


def _prep_inputs(L0_attn_x, L0_attn_wq, L0_attn_wk, L0_attn_wv, L0_attn_wo,
                 L0_attn_lambda_q1, L0_attn_lambda_k1, L0_attn_lambda_q2, L0_attn_lambda_k2,
                 L0_attn_subln_w, L0_ffn_gate_w, L0_ffn_Wg, L0_ffn_Wu, L0_ffn_Wd,
                 L0_ffn_cWg, L0_ffn_cWu, L0_ffn_cWd):
    f32 = lambda a: np.asarray(a, dtype=np.float32)

    def pre_qk(w):
        return np.ascontiguousarray(
            f32(w).reshape(16, 128, 16, 128).transpose(0, 3, 2, 1).reshape(16, 128, D))

    def pre_mov(w):
        return np.ascontiguousarray(f32(w).T.reshape(16, 128, D))

    wq = pre_qk(L0_attn_wq)
    wk = pre_qk(L0_attn_wk)
    wv = pre_qk(L0_attn_wv)
    wo = pre_mov(L0_attn_wo)
    gate = np.ascontiguousarray(f32(L0_ffn_gate_w).T.reshape(16, 128, 32))
    lam = np.concatenate([f32(L0_attn_lambda_q1), f32(L0_attn_lambda_k1),
                          f32(L0_attn_lambda_q2), f32(L0_attn_lambda_k2)]).reshape(1, 256)
    wg = np.ascontiguousarray(
        f32(L0_ffn_Wg).reshape(EL, 16, 128, 6, 128).transpose(0, 3, 2, 1, 4).reshape(EL, 6, 128, D))
    wu = np.ascontiguousarray(
        f32(L0_ffn_Wu).reshape(EL, 16, 128, 6, 128).transpose(0, 3, 2, 1, 4).reshape(EL, 6, 128, D))
    wd = np.ascontiguousarray(
        f32(L0_ffn_Wd).reshape(EL, 6, 128, 4, 512).transpose(0, 3, 2, 1, 4))
    cwg = np.ascontiguousarray(
        f32(L0_ffn_cWg).reshape(ES, 16, 128, 128).transpose(0, 2, 1, 3).reshape(ES, 128, D))
    cwu = np.ascontiguousarray(
        f32(L0_ffn_cWu).reshape(ES, 16, 128, 128).transpose(0, 2, 1, 3).reshape(ES, 128, D))
    cwd = np.ascontiguousarray(
        f32(L0_ffn_cWd).reshape(ES, 128, 4, 512).transpose(0, 2, 1, 3))
    common = dict(
        wq=wq, wk=wk, wv=wv, wo=wo, gate=gate, lam=lam,
        subln=f32(L0_attn_subln_w),
        eye=np.eye(128).astype(ml_dtypes.bfloat16),
        eyef=np.eye(128).astype(np.float32),
        wg=wg, wu=wu, wd=wd, cwg=cwg, cwu=cwu, cwd=cwd,
    )
    xs = f32(L0_attn_x)
    in_maps = []
    for c in range(8):
        m = dict(common)
        m["x0"] = np.ascontiguousarray(xs[c])
        in_maps.append(m)
    return in_maps


def kernel(**inputs):
    in_maps = _prep_inputs(**inputs)
    nc = build_program(debug=_DEBUG)
    res = run_bass_kernel_spmd(nc, in_maps, core_ids=list(range(8)))
    out = np.stack([np.asarray(r["out"], dtype=np.float32) for r in res.results])
    if _DEBUG:
        kernel._last = res
    return out
